# version 05
# baseline (speedup 1.0000x reference)
import numpy as np
from contextlib import ExitStack
import ml_dtypes
import concourse.bass as bass
import concourse.mybir as mybir
from concourse.bass_utils import run_bass_kernel_spmd
from concourse.alu_op_type import AluOpType as OP

F32 = mybir.dt.float32
BF16 = mybir.dt.bfloat16
AF = mybir.ActivationFunctionType
AX = mybir.AxisListType

MAGIC = float(np.float32(1.5 * 2 ** 23))
RD = float(np.float32(1.0) / np.float32(1024.0))
ESHIFT = -12.0
R127 = float(np.float32(1.0) / np.float32(127.0))
S, D, T, HID = 2048, 1024, 4096, 4096
NCORES = 8
STOP = "all"
DEBUG = False


class Ctr:
    __slots__ = ("h", "n")

    def __init__(self, h):
        self.h = h
        self.n = 0

    def inc(self, inst, k=1):
        inst.then_inc(self.h, k)
        self.n += k
        return (self, self.n)

    def tag(self):
        return (self, self.n)


def _w(eng, *tags):
    for t in tags:
        if t is not None and t[1] > 0:
            eng.wait_ge(t[0].h, t[1])


def build():
    nc = bass.Bass("TRN2", target_bir_lowering=False)

    x_h = nc.dram_tensor("x", [T, D], F32, kind="ExternalInput").ap()
    wq_h = nc.dram_tensor("wq", [D, D], F32, kind="ExternalInput").ap()
    wk_h = nc.dram_tensor("wk", [D, D], F32, kind="ExternalInput").ap()
    wv_h = nc.dram_tensor("wv", [D, D], F32, kind="ExternalInput").ap()
    wo_h = nc.dram_tensor("wo", [D, D], F32, kind="ExternalInput").ap()
    w1t_h = nc.dram_tensor("w1t", [D, HID], F32, kind="ExternalInput").ap()
    w2t_h = nc.dram_tensor("w2t", [HID, D], F32, kind="ExternalInput").ap()
    bq2_h = nc.dram_tensor("bq2", [128, 8], F32, kind="ExternalInput").ap()
    bk2_h = nc.dram_tensor("bk2", [128, 8], F32, kind="ExternalInput").ap()
    bv2_h = nc.dram_tensor("bv2", [128, 8], F32, kind="ExternalInput").ap()
    bor_h = nc.dram_tensor("bor", [1, D], F32, kind="ExternalInput").ap()
    b12_h = nc.dram_tensor("b12", [128, 32], F32, kind="ExternalInput").ap()
    b2r_h = nc.dram_tensor("b2r", [1, D], F32, kind="ExternalInput").ap()
    idb_h = nc.dram_tensor("idb", [128, 128], BF16, kind="ExternalInput").ap()
    idf_h = nc.dram_tensor("idf", [128, 128], F32, kind="ExternalInput").ap()
    mk_h = nc.dram_tensor("maskt", [128, 768], BF16, kind="ExternalInput").ap()
    out_h = nc.dram_tensor("out", [T, D], F32, kind="ExternalOutput").ap()
    if DEBUG:
        dbg_h = nc.dram_tensor("dbg", [128, 49408], BF16, kind="ExternalOutput").ap()
        dbgx_h = nc.dram_tensor("dbgx", [T, D], BF16, kind="ExternalOutput").ap()
        dbgf_h = nc.dram_tensor("dbgf", [128, 64], F32, kind="ExternalOutput").ap()
    hscr = nc.dram_tensor("hscr", [HID, T], F32)
    x1scr = nc.dram_tensor("x1scr", [T, D], BF16)
    cci1 = nc.dram_tensor("cci1", [1, 8], F32)
    cco1 = nc.dram_tensor("cco1", [1, 8], F32)
    cci2 = nc.dram_tensor("cci2", [1, 8], F32)
    cco2 = nc.dram_tensor("cco2", [1, 8], F32)

    es = ExitStack()
    with es:
        def sb(name, shape, dt, st=es):
            return st.enter_context(nc.sbuf_tensor(name, shape, dt))

        idb = sb("idb_s", [128, 128], BF16)
        idf = sb("idf_s", [128, 128], F32)
        maskt = sb("maskt_s", [128, 768], BF16)
        bq2 = sb("bq2_s", [128, 8], F32)
        bk2 = sb("bk2_s", [128, 8], F32)
        bv2 = sb("bv2_s", [128, 8], F32)
        bor = sb("bor_s", [1, D], F32)
        b12 = sb("b12_s", [128, 32], F32)
        b1s = sb("b1s", [128, 32], F32)
        bop = sb("boprime", [1, D], F32)
        onesf = sb("onesf", [1, 128], F32)
        esh = sb("esh", [128, 1], F32)
        rstd2a = sb("rstd2a", [128, 32], F32)
        negm2a = sb("negm2a", [128, 32], F32)
        redt = sb("redt", [1, 8], F32)
        gmt = sb("gmt", [1, 8], F32)
        sval = sb("sval", [1, 8], F32)
        bvec = sb("bvec", [128, 8], F32)
        lnv = [sb(f"lnv{i}", [128, 8], F32) for i in range(2)]
        svb = sb("svb", [1, 8], F32)
        rstd2b = sb("rstd2b", [128, 32], F32)
        negm2b = sb("negm2b", [128, 32], F32)
        spc = sb("spc", [128, 128], BF16)
        sA1 = sb("sA1", [128, 16], F32)
        qA1 = sb("qA1", [128, 16], F32)
        mA1 = sb("mA1", [128, 16], F32)
        eA1 = sb("eA1", [128, 16], F32)
        m2A1 = sb("m2A1", [128, 16], F32)
        veA1 = sb("veA1", [128, 16], F32)
        rvA1 = sb("rvA1", [128, 16], F32)
        rstdA1 = sb("rstdA1", [128, 16], F32)
        negmA1 = sb("negmA1", [128, 16], F32)
        sA2 = sb("sA2", [128, 32], F32)
        qA2 = sb("qA2", [128, 32], F32)
        mA2 = sb("mA2", [128, 32], F32)
        eA2 = sb("eA2", [128, 32], F32)
        m2A2 = sb("m2A2", [128, 32], F32)
        veA2 = sb("veA2", [128, 32], F32)
        rvA2 = sb("rvA2", [128, 32], F32)
        amxc = sb("amxc", [128, 32], F32)
        amhc = sb("amhc", [128, 256], F32)
        aw1c = sb("aw1c", [128, 96], F32)
        aw2c = sb("aw2c", [128, 96], F32)
        amx1 = sb("amx1", [128, 1], F32)
        amh1 = sb("amh1", [128, 1], F32)
        aw11 = sb("aw11", [128, 1], F32)
        aw21 = sb("aw21", [128, 1], F32)

        def mk(name):
            return Ctr(es.enter_context(nc.semaphore(name)))

        pe, ve, se, ge = mk("pe"), mk("ve"), mk("se"), mk("ge")
        s_x = [mk("s_x0"), mk("s_x1")]
        s_w = [mk("s_w0"), mk("s_w1")]
        s_b = mk("s_b")
        s_x1w = [mk("s_x1w0"), mk("s_x1w1")]
        s_x1r = [mk("s_x1r0"), mk("s_x1r1")]
        s_hw = [mk("s_hw0"), mk("s_hw1")]
        s_hr = [mk("s_hr0"), mk("s_hr1")]
        s_cc = mk("s_cc")
        s_wa = [mk("s_wa0"), mk("s_wa1")]
        s_out = [mk(f"s_out{i}") for i in range(4)]

        blk = es.enter_context(nc.Block())
        ENGS = {"t": blk.tensor, "v": blk.vector, "s": blk.scalar, "g": blk.gpsimd}
        ECTR = {"t": pe, "v": ve, "s": se, "g": ge}

        def barrier(dma_tags=()):
            tags = []
            for k in ("t", "v", "s", "g"):
                c = ECTR[k]

                def _f(e, c=c):
                    e.drain()
                    c.inc(e.nop())
                ENGS[k](_f)
                tags.append(c.tag())

            def _fy(e, tags=tags, dma_tags=tuple(dma_tags)):
                _w(e, *tags)
                for dt_ in dma_tags:
                    _w(e, dt_)
            for k in ("t", "v", "s", "g"):
                ENGS[k](_fy)
            blk.sync(_fy)

        def finish(dumps=(), hbm_dumps=(), fdumps=()):
            def _d(e):
                if DEBUG:
                    off = 0
                    for srct in dumps:
                        w = srct.shape[1]
                        s_out[0].inc(e.dma_start(out=dbg_h[:, off:off + w], in_=srct[:]), 16)
                        off += w
                    for dst, srcafé in hbm_dumps:
                        s_out[0].inc(e.dma_start(out=dst[:, :], in_=srcafé[:, :]), 16)
                    off = 0
                    for srct in fdumps:
                        w = srct.shape[1]
                        s_out[0].inc(e.dma_start(out=dbgf_h[:, off:off + w], in_=srct[:]), 16)
                        off += w
                _w(e, s_out[0].tag())
            blk.sync(_d)

        scar = sb("scar", [128, 1], F32)

        def sfence(s):
            s.drain()
            return se.inc(s.activation(out=scar[:], in_=esh[:], func=AF.Identity))

        def vfence2(v):
            vfence(v)
            return vfence(v)

        _vfk = [0]

        def vcar(v, inc=True):
            o = 8 * (_vfk[0] % 16)
            _vfk[0] += 1
            ii = v.tensor_copy(out=spc[:, o:o + 8], in_=maskt[:, o:o + 8])
            if inc:
                return ve.inc(ii)
            return None

        def vfence(v):
            v.drain()
            return vcar(v)

        def _pre_y(e):
            for dst, src in ((idb, idb_h), (idf, idf_h), (maskt, mk_h),
                             (bq2, bq2_h), (bk2, bk2_h), (bv2, bv2_h),
                             (bor, bor_h), (b12, b12_h)):
                _w(e, s_b.tag())
                s_b.inc(e.dma_start(out=dst[:], in_=src[:, :]), 16)
        blk.sync(_pre_y)

        def _pre_v(v):
            _w(v, s_b.tag())
            v.memset(onesf[:], 1.0)
            v.memset(esh[:], ESHIFT)
            ve.inc(v.memset(spc[:], 0.0))
        blk.vector(_pre_v)

        aes = es.enter_context(ExitStack())
        Qs = sb("Qs", [128, 8 * S], BF16, aes)
        Ks = sb("Ks", [128, 8 * S], BF16, aes)
        Vs = sb("Vs", [128, 16 * 1040], BF16, aes)
        ln1T = sb("ln1T", [128, 8 * S], BF16, aes)
        AOs = ln1T
        wstage = [sb(f"wstg{i}", [128, 512], F32, aes) for i in range(2)]
        wb = sb("wb", [128, 8 * 1024], BF16, aes)
        xin = [sb(f"xin{i}", [128, 1024], F32, aes) for i in range(2)]
        wabs = [sb(f"wabs{i}", [128, 384], F32, aes) for i in range(2)]
        lntl = [sb(f"lntl{i}", [128, 1024], BF16, aes) for i in range(2)]
        ptsb = [sb(f"ptsb{i}", [128, 768], BF16, aes) for i in range(2)]
        qz = [sb(f"qz{i}", [128, 2 * S], BF16, aes) for i in range(2)]
        ptm = [sb(f"ptm{i}", [128, 768], BF16, aes) for i in range(2)]
        rrt = [sb(f"rrt{i}", [128, 2], F32, aes) for i in range(2)]
        x1t = [sb(f"x1t{i}", [128, 1024], BF16, aes) for i in range(2)]

        aps = es.enter_context(ExitStack())
        P0 = aps.enter_context(nc.psum_tensor("P0", [128, 512], BF16))
        P0b = aps.enter_context(nc.psum_tensor("P0b", [128, 512], BF16))
        SC = [aps.enter_context(nc.psum_tensor("SC0", [128, 1024], F32)),
              aps.enter_context(nc.psum_tensor("SC1", [128, 1024], F32))]
        AT = [aps.enter_context(nc.psum_tensor("AT0", [128, 512], F32)),
              aps.enter_context(nc.psum_tensor("AT1", [128, 512], F32))]

        def _vones(v):
            ve.inc(v.memset(Vs[:, 64::65], 1.0))
        blk.vector(_vones)
        barrier(dma_tags=(s_b.tag(),))

        if STOP == "pre":
            barrier()
            finish()
            return nc

        wa_free = [None, None]
        wa_cols = []
        _tiles = [(w1t_h, 128 * dc, 384 * q, 0, 0)
                  for dc in range(8) for q in range(10)]
        _tiles += [(w2t_h, 128 * hc, 384 * q, 1, 0)
                   for hc in range(32) for q in range(2)]
        _tiles = []
        col = 0
        for dc in range(8):
            for q in range(11):
                c0 = 384 * q
                w = min(384, 4096 - c0)
                if w <= 0:
                    continue
                _tiles.append((w1t_h, 128 * dc, c0, w, 0, col)); col += 1
        ncol1 = col
        col = 0
        for hc in range(32):
            for q in range(3):
                c0 = 384 * q
                w = min(384, 1024 - c0)
                if w <= 0:
                    continue
                _tiles.append((w2t_h, 128 * hc, c0, w, 1, col)); col += 1
        ncol2 = col
        for wi, (hsrc, r0, c0, w_, which, col) in enumerate(_tiles):
            slw = wi % 2
            ltw = [None]

            def _lwa(g, slw=slw, hsrc=hsrc, r0=r0, c0=c0, w_=w_, ltw=ltw):
                _w(g, wa_free[slw], s_wa[slw].tag())
                ltw[0] = s_wa[slw].inc(
                    g.dma_start(out=wabs[slw][:, 0:w_], in_=hsrc[r0:r0 + 128, c0:c0 + w_]), 16)
            blk.gpsimd(_lwa)

            def _rwa(v, slw=slw, which=which, col=col, w_=w_, ltw=ltw):
                _w(v, ltw[0])
                dst = aw1c if which == 0 else aw2c
                v.tensor_reduce(out=dst[:, col:col + 1], in_=wabs[slw][:, 0:w_], axis=AX.X,
                                op=OP.max, apply_absolute_value=True)
                vfence(v)
            blk.vector(_rwa)
            wa_free[slw] = ve.tag()
        wabs_done = ve.tag()

        tp_state = {"free": [None, None]}

        def transpose_tile(src, dstT, W, toff, dep_tags, pbanks):
            last = None
            for hh in range(2):
                ps = pbanks[hh]
                t_half = [None]

                def _tr(tn, ps=ps, hh=hh, t_half=t_half, dep_tags=tuple(dep_tags)):
                    _w(tn, *dep_tags, tp_state["free"][hh])
                    for j in range(4):
                        ii = tn.transpose(out=ps[:, 128 * j:128 * (j + 1)],
                                          in_=src[:, 128 * (4 * hh + j):128 * (4 * hh + j + 1)],
                                          identity=idb[:])
                    t_half[0] = pe.inc(ii)
                blk.tensor(_tr)

                def _ev(v, ps=ps, hh=hh, t_half=t_half):
                    _w(v, t_half[0])
                    v.tensor_copy(
                        out=dstT[:].rearrange("p (c n) -> p c n", n=W)[
                            :, 4 * hh:4 * hh + 4, toff:toff + 128],
                        in_=ps[:].rearrange("p (c n) -> p c n", n=128))
                    vfence(v)
                blk.vector(_ev)
                tp_state["free"][hh] = ve.tag()
                last = ve.tag()
            return last

        xin_free = [None, None]
        xin_free2 = [None, None]
        lnt_free = [None, None]
        at_free = [None, None]
        x1_free = [None, None]
        wb_free = None

        def load_w(src_h, dst, wb_free):
                wst_free = [None, None]
                last = [None]
                for ch in range(16):
                    dc, hf = ch // 2, ch % 2
                    sl = ch % 2
                    lt = [None]

                    def _lw(e, sl=sl, dc=dc, hf=hf, lt=lt, src_h=src_h):
                        _w(e, wst_free[sl])
                        _w(e, s_w[sl].tag())
                        lt[0] = s_w[sl].inc(
                            e.dma_start(out=wstage[sl][:],
                                        in_=src_h[128 * dc:128 * (dc + 1),
                                                  512 * hf:512 * (hf + 1)]), 16)
                    blk.scalar(_lw)

                    def _cw(v, sl=sl, dc=dc, hf=hf, ch=ch, lt=lt, dst=dst,
                            wb_free=wb_free, last=last):
                        _w(v, lt[0], wb_free if ch == 0 else None)
                        v.tensor_copy(out=dst[:, 1024 * dc + 512 * hf:1024 * dc + 512 * hf + 512],
                                      in_=wstage[sl][:])
                        if ch % 2 == 1 or ch == 15:
                            last[0] = vfence(v)
                    blk.vector(_cw)
                    if ch % 2 == 1 or ch == 15:
                        wst_free[0] = ve.tag()
                        wst_free[1] = ve.tag()
                return last[0]

        for sq in range(2):
            tok0 = sq * S
            wtag_q = load_w(wq_h, wb, wb_free)

            for t in range(16):
                sl = t % 2
                row = tok0 + 128 * t
                ld_tag = [None]

                def _ld(e, sl=sl, row=row, ld_tag=ld_tag):
                    _w(e, xin_free[sl], xin_free2[sl])
                    _w(e, s_x[sl].tag())
                    ld_tag[0] = s_x[sl].inc(
                        e.dma_start(out=xin[sl][:], in_=x_h[row:row + 128, :]), 16)
                blk.sync(_ld)

                def _rs(v, sl=sl, t=t, ld_tag=ld_tag):
                    _w(v, ld_tag[0])
                    v.tensor_reduce(out=sA1[:, t:t + 1], in_=xin[sl][:],
                                    axis=AX.X, op=OP.add)
                    vfence2(v)
                blk.vector(_rs)
                xin_free[sl] = ve.tag()

                def _qs(s, sl=sl, t=t, ld_tag=ld_tag):
                    _w(s, ld_tag[0])
                    se.inc(s.activation(out=lntl[sl][:], in_=xin[sl][:], func=AF.Square,
                                        accum_out=qA1[:, t:t + 1]))
                blk.scalar(_qs)
                xin_free2[sl] = se.tag()

            bm_tag = [None]

            def _bm(v):
                _w(v, xin_free[0], xin_free[1], xin_free2[0], xin_free2[1])
                v.tensor_scalar(out=mA1[:], in0=sA1[:], scalar1=RD, scalar2=None, op0=OP.mult)
                v.tensor_scalar(out=eA1[:], in0=qA1[:], scalar1=RD, scalar2=1e-5,
                                op0=OP.mult, op1=OP.add)
                v.drain()
                v.tensor_tensor(out=m2A1[:], in0=mA1[:], in1=mA1[:], op=OP.mult)
                v.drain()
                v.scalar_tensor_tensor(out=veA1[:], in0=m2A1[:], scalar=-1.0, in1=eA1[:],
                                       op0=OP.mult, op1=OP.add)
                v.drain()
                v.reciprocal(out=rvA1[:], in_=veA1[:])
                bm_tag[0] = vfence(v)
            blk.vector(_bm)

            sq_tag = [None]

            def _bs(s):
                _w(s, bm_tag[0], xin_free2[0], xin_free2[1])
                s.activation(out=rstdA1[:], in_=rvA1[:], func=AF.Sqrt)
                sq_tag[0] = sfence(s)
            blk.scalar(_bs)

            ng_tag = [None]

            def _bn2(v):
                _w(v, sq_tag[0])
                v.scalar_tensor_tensor(out=negmA1[:], in0=mA1[:], scalar=-1.0, in1=rstdA1[:],
                                       op0=OP.mult, op1=OP.mult)
                ng_tag[0] = vfence(v)
            blk.vector(_bn2)

            for t in range(16):
                sl = t % 2
                row = tok0 + 128 * t
                ld_tag = [None]

                def _ld2(e, sl=sl, row=row, ld_tag=ld_tag):
                    _w(e, xin_free[sl], xin_free2[sl])
                    _w(e, s_x[sl].tag())
                    ld_tag[0] = s_x[sl].inc(
                        e.dma_start(out=xin[sl][:], in_=x_h[row:row + 128, :]), 16)
                blk.sync(_ld2)

                ap_tag = [None]

                def _ap(s, sl=sl, t=t, ld_tag=ld_tag, ap_tag=ap_tag):
                    _w(s, ld_tag[0], ng_tag[0], lnt_free[sl])
                    s.activation(out=lntl[sl][:], in_=xin[sl][:],
                                 func=AF.Identity,
                                 bias=negmA1[:, t:t + 1],
                                 scale=rstdA1[:, t:t + 1])
                    ap_tag[0] = sfence(s)
                blk.scalar(_ap)
                xin_free[sl] = se.tag()
                lnt_free[sl] = transpose_tile(lntl[sl], ln1T, S, 128 * t,
                                              (ap_tag[0],), (P0, P0b))

            barrier()
            if STOP == "A0":
                finish(dumps=(ln1T,))
                return nc



            def proj_featmajor(dst, bias2d, wtag):
                nonlocal wb_free
                it = 0
                for hp in range(8):
                    for tq in range(4):
                        sl = it % 2
                        mt = [None]

                        def _mm(tn, hp=hp, tq=tq, sl=sl, mt=mt, wtag=wtag):
                            _w(tn, wtag, at_free[sl])
                            for dc in range(8):
                                ii = tn.matmul(
                                    out=AT[sl][:],
                                    lhsT=wb[:, 1024 * dc + 128 * hp:1024 * dc + 128 * hp + 128],
                                    rhs=ln1T[:, S * dc + 512 * tq:S * dc + 512 * tq + 512],
                                    start=(dc == 0), stop=(dc == 7))
                            mt[0] = pe.inc(ii)
                        blk.tensor(_mm)

                        def _evq(s, hp=hp, tq=tq, sl=sl, mt=mt, dst=dst, bias2d=bias2d):
                            _w(s, mt[0])
                            if bias2d is not None:
                                s.activation(
                                    out=dst[:, S * hp + 512 * tq:S * hp + 512 * tq + 512],
                                    in_=AT[sl][:], func=AF.Identity,
                                    bias=bias2d[:, hp:hp + 1], scale=1.0)
                            else:
                                s.activation(
                                    out=dst[:, S * hp + 512 * tq:S * hp + 512 * tq + 512],
                                    in_=AT[sl][:], func=AF.Identity)
                            sfence(s)
                        blk.scalar(_evq)
                        at_free[sl] = se.tag()
                        it += 1
                wb_free = pe.tag()

            proj_featmajor(Qs, bq2, wtag_q)
            barrier()
            if STOP == "Bq":
                barrier()
                finish(dumps=(Qs,))
                return nc
            wtag = load_w(wk_h, wb, wb_free)
            proj_featmajor(Ks, bk2, wtag)
            barrier()
            if STOP == "Bk":
                barrier()
                finish(dumps=(Qs, Ks))
                return nc
            wtag = load_w(wv_h, wb, wb_free)
            it = 0
            for tt in range(16):
                for hf in range(2):
                    sl = it % 2
                    mt = [None]

                    def _mmv(tn, tt=tt, hf=hf, sl=sl, mt=mt, wtag=wtag):
                        _w(tn, wtag, at_free[sl])
                        for dc in range(8):
                            ii = tn.matmul(
                                out=AT[sl][:],
                                lhsT=ln1T[:, S * dc + 128 * tt:S * dc + 128 * tt + 128],
                                rhs=wb[:, 1024 * dc + 512 * hf:1024 * dc + 512 * hf + 512],
                                start=(dc == 0), stop=(dc == 7))
                        mt[0] = pe.inc(ii)
                    blk.tensor(_mmv)

                    def _evv(s, tt=tt, hf=hf, sl=sl, mt=mt):
                        _w(s, mt[0])
                        s.activation(
                            out=Vs[:].rearrange("p (a b) -> p a b", b=65)[
                                :, 16 * tt + 8 * hf:16 * tt + 8 * hf + 8, 0:64],
                            in_=AT[sl][:].rearrange("p (a b) -> p a b", b=64),
                            func=AF.Identity)
                        sfence(s)
                    blk.scalar(_evv)
                    at_free[sl] = se.tag()
                    it += 1
            wb_free = pe.tag()
            barrier()

            wst_free2 = [None, None]
            bo_mm = [None]
            for dc in range(8):
                for hf in range(2):
                    sl = (2 * dc + hf) % 2
                    lt = [None]

                    def _lw(e, sl=sl, dc=dc, hf=hf, lt=lt):
                        for tg in (wst_free2[sl] or ()):
                            _w(e, tg)
                        _w(e, s_w[sl].tag())
                        lt[0] = s_w[sl].inc(
                            e.dma_start(out=wstage[sl][:],
                                        in_=wo_h[128 * dc:128 * (dc + 1),
                                                 512 * hf:512 * (hf + 1)]), 16)
                    blk.sync(_lw)

                    ct_ = [None]

                    def _cw(v, sl=sl, dc=dc, hf=hf, lt=lt, ct_=ct_):
                        _w(v, lt[0], wb_free if dc == 0 and hf == 0 else None)
                        v.tensor_copy(
                            out=wb[:, 1024 * dc + 512 * hf:1024 * dc + 512 * hf + 512],
                            in_=wstage[sl][:])
                        ct_[0] = vfence(v)
                    blk.vector(_cw)

                    if sq == 0:
                        def _bo(tn, sl=sl, dc=dc, hf=hf, lt=lt, bo_mm=bo_mm):
                            _w(tn, lt[0])
                            bo_mm[0] = pe.inc(tn.matmul(
                                out=SC[0][0:1, 512 * hf:512 * hf + 512],
                                lhsT=bv2[:, dc:dc + 1],
                                rhs=wstage[sl][:, 0:512], start=(dc == 0), stop=(dc == 7)))
                        blk.tensor(_bo)
                        wst_free2[sl] = (ct_[0], pe.tag())
                    else:
                        wst_free2[sl] = (ct_[0],)
            wotag = ve.tag()

            if sq == 0:
                def _boev(v, bo_mm=bo_mm):
                    _w(v, bo_mm[0])
                    v.tensor_tensor(out=bop[:], in0=SC[0][0:1, :], in1=bor[:],
                                    op=OP.add)
                    vfence(v)
                blk.vector(_boev)

            barrier()
            if STOP == "B0":
                finish(dumps=(Qs, Ks, Vs))
                return nc

            sc_free = [None, None]
            pt_free = [None, None]
            ptm_free = [None, None]
            atp_free = [None, None]
            prev_ev = [None]
            qz_free = [None, None]
            qz_tag = [None, None]
            it = 0
            for hp in range(8):
                qsl = hp % 2
                qztag = [None]

                def _qz(v, hp=hp, qsl=qsl, qztag=qztag):
                    _w(v, qz_free[qsl])
                    v.tensor_copy(out=qz[qsl][0:64, 0:S], in_=Qs[0:64, S * hp:S * (hp + 1)])
                    v.memset(qz[qsl][64:128, 0:S], 0.0)
                    v.tensor_copy(out=qz[qsl][64:128, S:2 * S],
                                  in_=Qs[64:128, S * hp:S * (hp + 1)])
                    v.memset(qz[qsl][0:64, S:2 * S], 0.0)
                    qztag[0] = vfence(v)
                blk.vector(_qz)
                qz_tag[qsl] = qztag[0]

                for qb in range(16):
                    sl = it % 2
                    if qb == 0:
                        tcs, moff = [0, 1], 128
                    elif qb == 15:
                        tcs, moff = [14, 15], 0
                    else:
                        tcs, moff = [qb - 1, qb, qb + 1], None
                    nch = len(tcs)
                    smt = [None]

                    def _sco(tn, hp=hp, qb=qb, sl=sl, qsl=qsl, tcs=tcs, smt=smt,
                             qztag=qztag):
                        _w(tn, sc_free[sl], qztag[0] if qb == 0 else None)
                        for hh in range(2):
                            for ci in range(len(tcs)):
                                tc = tcs[ci]
                                ii = tn.matmul(
                                    out=SC[sl][:, 384 * hh + 128 * ci:384 * hh + 128 * ci + 128],
                                    lhsT=Ks[:, S * hp + 128 * tc:S * hp + 128 * tc + 128],
                                    rhs=qz[qsl][:, S * hh + 128 * qb:S * hh + 128 * qb + 128],
                                    start=True, stop=True)
                        smt[0] = pe.inc(ii)
                    blk.tensor(_sco)

                    ext = [None]

                    def _exp(s, sl=sl, nch=nch, smt=smt, ext=ext):
                        _w(s, smt[0], pt_free[sl])
                        if nch == 3:
                            ii = s.activation(out=ptsb[sl][:], in_=SC[sl][:, 0:768],
                                              func=AF.Exp, bias=esh[:], scale=0.125)

                        else:
                            s.activation(out=ptsb[sl][:, 0:256], in_=SC[sl][:, 0:256],
                                         func=AF.Exp, bias=esh[:], scale=0.125)
                            ii = s.activation(out=ptsb[sl][:, 384:640], in_=SC[sl][:, 384:640],
                                              func=AF.Exp, bias=esh[:], scale=0.125)
                        ext[0] = sfence(s)
                    blk.scalar(_exp)
                    sc_free[sl] = se.tag()

                    mkt = [None]

                    def _msk(v, sl=sl, nch=nch, moff=moff, ext=ext, mkt=mkt):
                        _w(v, ext[0], ptm_free[sl])
                        if nch == 3:
                            v.tensor_tensor(out=ptm[sl][:], in0=ptsb[sl][:],
                                            in1=maskt[:], op=OP.mult)
                        else:
                            po = ptm[sl][:].rearrange("p (a b) -> p a b", b=384)[:, :, 0:256]
                            pv = ptsb[sl][:].rearrange("p (a b) -> p a b", b=384)[:, :, 0:256]
                            mv = maskt[:].rearrange("p (a b) -> p a b", b=384)[:, :, moff:moff + 256]
                            v.tensor_tensor(out=po, in0=pv, in1=mv, op=OP.mult)
                        pv_ = prev_ev[0]
                        if pv_ is not None:
                            php, pqb, psl = pv_
                            vcar(v, inc=False)
                            v.tensor_scalar(
                                out=AOs[:, 1024 * pqb + 128 * php:1024 * pqb + 128 * php + 64],
                                in0=AT[psl][:, 0:64], scalar1=rrt[psl][:, 0:1],
                                scalar2=None, op0=OP.mult)
                            v.tensor_scalar(
                                out=AOs[:, 1024 * pqb + 128 * php + 64:1024 * pqb + 128 * php + 128],
                                in0=AT[psl][:, 65:129], scalar1=rrt[psl][:, 1:2],
                                scalar2=None, op0=OP.mult)
                        v.drain()
                        mkt[0] = vcar(v)
                        if pv_ is not None:
                            atp_free[pv_[2]] = ve.tag()
                        prev_ev[0] = None
                    blk.vector(_msk)

                    pvt = [None]

                    pt_free[sl] = ve.tag()

                    def _pv(tn, hp=hp, sl=sl, tcs=tcs, mkt=mkt, pvt=pvt):
                        _w(tn, mkt[0], atp_free[sl])
                        for hh in range(2):
                            h = 2 * hp + hh
                            for ci in range(len(tcs)):
                                tc = tcs[ci]
                                ii = tn.matmul(
                                    out=AT[sl][:, 65 * hh:65 * hh + 65],
                                    lhsT=ptm[sl][:, 384 * hh + 128 * ci:384 * hh + 128 * ci + 128],
                                    rhs=Vs[:, 1040 * tc + 65 * h:1040 * tc + 65 * h + 65],
                                    start=(ci == 0), stop=(ci == len(tcs) - 1))
                        pvt[0] = pe.inc(ii)
                    blk.tensor(_pv)
                    ptm_free[sl] = pe.tag()

                    def _aev(v, hp=hp, qb=qb, sl=sl, pvt=pvt):
                        _w(v, pvt[0])
                        v.reciprocal(out=rrt[sl][:], in_=AT[sl][:, 64:130:65])
                        vfence(v)
                        prev_ev[0] = (hp, qb, sl)
                    blk.vector(_aev)
                    it += 1
                qz_free[qsl] = pe.tag()

            def _aev_fl(v):
                pv = prev_ev[0]
                php, pqb, psl = pv
                v.drain()
                v.tensor_scalar(
                    out=AOs[:, 1024 * pqb + 128 * php:1024 * pqb + 128 * php + 64],
                    in0=AT[psl][:, 0:64], scalar1=rrt[psl][:, 0:1],
                    scalar2=None, op0=OP.mult)
                v.tensor_scalar(
                    out=AOs[:, 1024 * pqb + 128 * php + 64:1024 * pqb + 128 * php + 128],
                    in0=AT[psl][:, 65:129], scalar1=rrt[psl][:, 1:2],
                    scalar2=None, op0=OP.mult)
                vfence(v)
                prev_ev[0] = None
            blk.vector(_aev_fl)

            barrier()
            if STOP == "C0":
                finish(dumps=(AOs,))
                return nc

            for t in range(16):
                sl = t % 2
                row = tok0 + 128 * t
                ld_tag = [None]

                def _ld(e, sl=sl, row=row, ld_tag=ld_tag):
                    _w(e, xin_free[sl])
                    _w(e, s_x[sl].tag())
                    ld_tag[0] = s_x[sl].inc(
                        e.dma_start(out=xin[sl][:], in_=x_h[row:row + 128, :]), 16)
                blk.sync(_ld)

                aot_tag = transpose_tile(AOs[:, 1024 * t:1024 * (t + 1)], lntl[sl], 128, 0,
                                         (lnt_free[sl],), (P0, P0b))
                om_last = None
                for hf in range(2):
                    psl = hf
                    mt = [None]

                    def _om(tn, hf=hf, sl=sl, psl=psl, mt=mt, aot_tag=aot_tag, t=t):
                        _w(tn, aot_tag, at_free[psl], wotag if t == 0 else None)
                        for dc in range(8):
                            tn.matmul(out=AT[psl][:],
                                      lhsT=lntl[sl][:, 128 * dc:128 * (dc + 1)],
                                      rhs=wb[:, 1024 * dc + 512 * hf:1024 * dc + 512 * hf + 512],
                                      start=(dc == 0), stop=False)
                        mt[0] = pe.inc(tn.matmul(
                            out=AT[psl][:], lhsT=onesf[0:1, :],
                            rhs=bop[0:1, 512 * hf:512 * hf + 512], start=False, stop=True))
                    blk.tensor(_om)

                    def _orv(v, hf=hf, sl=sl, psl=psl, mt=mt, ld_tag=ld_tag):
                        _w(v, mt[0], ld_tag[0], x1_free[sl] if hf == 0 else None)
                        v.tensor_tensor(
                            out=x1t[sl][:, 512 * hf:512 * hf + 512], in0=AT[psl][:],
                            in1=xin[sl][:, 512 * hf:512 * hf + 512], op=OP.add)
                        vfence(v)
                    blk.vector(_orv)
                    at_free[psl] = ve.tag()
                lnt_free[sl] = pe.tag()
                xin_free[sl] = ve.tag()

                ti = sq * 16 + t

                def _st2(v, sl=sl, ti=ti):
                    v.tensor_reduce(out=sA2[:, ti:ti + 1], in_=x1t[sl][:],
                                    axis=AX.X, op=OP.add)
                    vfence2(v)
                blk.vector(_st2)
                red_done = ve.tag()

                def _sq2(s, sl=sl, ti=ti, red_done=red_done):
                    _w(s, red_done)
                    s.activation(out=xin[sl][:], in_=x1t[sl][:], func=AF.Square,
                                 accum_out=qA2[:, ti:ti + 1])
                    sfence(s)
                blk.scalar(_sq2)
                stats_done = se.tag()
                xin_free[sl] = se.tag()

                def _sp(g, sl=sl, row=row, stats_done=stats_done):
                    _w(g, stats_done)
                    _w(g, s_x1w[sl].tag())
                    s_x1w[sl].inc(g.dma_start(out=x1scr[row:row + 128, :], in_=x1t[sl][:]), 16)
                blk.gpsimd(_sp)
                x1_free[sl] = s_x1w[sl].tag()

            barrier(dma_tags=(s_x1w[0].tag(), s_x1w[1].tag()))
            if STOP == "D0" and sq == 0:
                finish(hbm_dumps=((dbgx_h, x1scr),) if DEBUG else ())
                return nc

        if STOP == "attn":
            def _fatt(e):
                if DEBUG:
                    s_out[0].inc(e.dma_start(out=dbgx_h[:, :], in_=x1scr[:, :]), 16)
                _w(e, s_out[0].tag())
            blk.sync(_fatt)
            aps.close()
            aes.close()
            return nc

        aps.close()
        aes.close()

        fesb = es.enter_context(ExitStack())
        x1r = [sb(f"x1r{i}", [128, 1024], BF16, fesb) for i in range(2)]
        ln2f = [sb(f"ln2f{i}", [128, 1024], F32, fesb) for i in range(2)]
        w2st = [sb(f"w2st{i}", [128, 1024], F32, fesb) for i in range(2)]
        fes = es.enter_context(ExitStack())
        w1st = [sb(f"w1st{i}", [128, 4096], F32, fes) for i in range(2)]
        w1qT = sb("w1qT", [128, 8 * 4096], BF16, fes)
        xqT = sb("xqT", [128, 8 * 4096], BF16, fes)
        xqtl = [sb(f"xqtl{i}", [128, 1024], BF16, fes) for i in range(2)]
        htl = [sb(f"htl{i}", [128, 512], F32, fes) for i in range(2)]

        fps = es.enter_context(ExitStack())
        PF0 = fps.enter_context(nc.psum_tensor("PF0", [128, 512], BF16))
        PF0b = fps.enter_context(nc.psum_tensor("PF0b", [128, 512], BF16))
        MM = [fps.enter_context(nc.psum_tensor(f"MM{i}", [128, 512], F32)) for i in range(3)]
        UP = fps.enter_context(nc.psum_tensor("UP", [128, 128], F32))
        tp_state["free"] = [None, None]

        bm2_tag = [None]

        def _bm2(v):
            v.tensor_scalar(out=mA2[:], in0=sA2[:], scalar1=RD, scalar2=None, op0=OP.mult)
            v.tensor_scalar(out=eA2[:], in0=qA2[:], scalar1=RD, scalar2=1e-5,
                            op0=OP.mult, op1=OP.add)
            v.drain()
            v.tensor_tensor(out=m2A2[:], in0=mA2[:], in1=mA2[:], op=OP.mult)
            v.drain()
            v.scalar_tensor_tensor(out=veA2[:], in0=m2A2[:], scalar=-1.0, in1=eA2[:],
                                   op0=OP.mult, op1=OP.add)
            v.drain()
            v.reciprocal(out=rvA2[:], in_=veA2[:])
            bm2_tag[0] = vfence(v)
        blk.vector(_bm2)

        fin_tag = [None]

        def _fin_s(s):
            _w(s, bm2_tag[0])
            s.activation(out=rstd2b[:], in_=rvA2[:], func=AF.Sqrt)
            fin_tag[0] = sfence(s)
        blk.scalar(_fin_s)

        ln2fin = [None]

        def _fin_v(v):
            _w(v, fin_tag[0])
            v.scalar_tensor_tensor(out=negm2b[:], in0=mA2[:], scalar=-1.0, in1=rstd2b[:],
                                   op0=OP.mult, op1=OP.mult)
            ln2fin[0] = vfence(v)
        blk.vector(_fin_v)

        x1r_free = [None, None]
        ln2f_free = [None, None]
        for t in range(32):
            sl = t % 2
            lt = [None]

            def _l(e, sl=sl, t=t, lt=lt):
                _w(e, x1r_free[sl])
                _w(e, s_x1r[sl].tag())
                lt[0] = s_x1r[sl].inc(
                    e.dma_start(out=x1r[sl][:], in_=x1scr[128 * t:128 * (t + 1), :]), 16)
            blk.sync(_l)

            apt = [None]

            def _a(s, sl=sl, t=t, lt=lt, apt=apt):
                _w(s, lt[0], ln2fin[0], ln2f_free[sl])
                s.activation(out=ln2f[sl][:], in_=x1r[sl][:], func=AF.Identity,
                             bias=negm2b[:, t:t + 1], scale=rstd2b[:, t:t + 1])
                apt[0] = sfence(s)
            blk.scalar(_a)
            x1r_free[sl] = se.tag()

            def _m(v, sl=sl, t=t, apt=apt):
                _w(v, apt[0])
                v.tensor_reduce(out=amxc[:, t:t + 1], in_=ln2f[sl][:], axis=AX.X,
                                op=OP.max, apply_absolute_value=True)
                vfence(v)
            blk.vector(_m)
            ln2f_free[sl] = ve.tag()

        def _mfin(v):
            v.drain()
            v.tensor_reduce(out=amx1[:], in_=amxc[:], axis=AX.X, op=OP.max)
            vfence2(v)
        blk.vector(_mfin)
        amx_done = ve.tag()

        def _w1fin(v):
            _w(v, wabs_done)
            vfence(v)
            _w(v, ve.tag())
            v.tensor_reduce(out=aw11[:], in_=aw1c[:, 0:ncol1], axis=AX.X, op=OP.max)
            vfence2(v)
        blk.vector(_w1fin)
        aw1_done = ve.tag()

        def _w2fin(v):
            _w(v, ve.tag())
            v.tensor_reduce(out=aw21[:], in_=aw2c[:, 0:ncol2], axis=AX.X, op=OP.max)
            vfence2(v)
        blk.vector(_w2fin)
        aw2_done = ve.tag()

        def part_max(src, dstcol, dep):
            tt_ = [None]

            def _t(tn, tt_=tt_, dep=dep):
                _w(tn, *dep)
                tt_[0] = pe.inc(tn.transpose(out=UP[0:1, 0:128], in_=src, identity=idf[:]))
            blk.tensor(_t)

            def _r(v, tt_=tt_, dstcol=dstcol):
                _w(v, tt_[0])
                v.tensor_reduce(out=sval[0:1, dstcol:dstcol + 1], in_=UP[0:1, 0:128],
                                axis=AX.X, op=OP.max)
                vfence2(v)
            blk.vector(_r)
            return ve.tag()

        t1 = part_max(amx1[:], 0, (amx_done,))
        t2 = part_max(aw11[:], 1, (aw1_done, t1))
        t3 = part_max(aw21[:], 2, (aw2_done, t2))

        cc1_tag = [None]

        def _prep_cc2(v):
            _w(v, t3)
            for j in range(8):
                v.tensor_copy(out=redt[0:1, j:j + 1], in_=sval[0:1, 0:1])
            cc1_tag[0] = vfence(v)
        blk.vector(_prep_cc2)

        gm_tag = [None]

        def _cc1(g):
            _w(g, cc1_tag[0])
            _w(g, s_cc.tag())
            s_cc.inc(g.dma_start(out=cci1[:, :], in_=redt[:]), 16)
            _w(g, s_cc.tag())
            ge.inc(g.collective_compute(
                "AllReduce", mybir.AluOpType.max,
                replica_groups=[list(range(NCORES))],
                ins=[cci1.ap().opt()], outs=[cco1.ap().opt()]))
            _w(g, ge.tag())
            _w(g, s_cc.tag())
            gm_tag[0] = s_cc.inc(g.dma_start(out=gmt[:], in_=cco1[:, :]), 16)
        blk.gpsimd(_cc1)

        sc_tag = [None]

        def _scl(v):
            _w(v, gm_tag[0])
            def SP():
                v.drain()
            v.tensor_reduce(out=svb[0:1, 0:1], in_=gmt[:], axis=AX.X, op=OP.max)
            _w(v, vfence2(v))
            v.tensor_scalar(out=svb[0:1, 1:2], in0=sval[0:1, 1:2], scalar1=1e-6,
                            scalar2=R127, op0=OP.max, op1=OP.mult)
            v.tensor_scalar(out=svb[0:1, 2:3], in0=sval[0:1, 2:3], scalar1=1e-6,
                            scalar2=R127, op0=OP.max, op1=OP.mult)
            v.tensor_scalar(out=svb[0:1, 3:4], in0=svb[0:1, 0:1], scalar1=1e-6,
                            scalar2=R127, op0=OP.max, op1=OP.mult)
            _w(v, vfence(v))
            v.tensor_copy(out=sval[0:1, 3:4], in_=svb[0:1, 3:4])
            v.reciprocal(out=sval[0:1, 4:5], in_=svb[0:1, 3:4])
            v.tensor_tensor(out=sval[0:1, 5:6], in0=svb[0:1, 1:2], in1=svb[0:1, 3:4],
                            op=OP.mult)
            v.reciprocal(out=sval[0:1, 6:7], in_=svb[0:1, 1:2])
            v.reciprocal(out=sval[0:1, 7:8], in_=svb[0:1, 2:3])
            v.drain()
            sc_tag[0] = ve.inc(v.tensor_copy(out=svb[0:1, 7:8], in_=sval[0:1, 7:8]))
        blk.vector(_scl)

        bc_tag = [None]

        def _bc(tn):
            _w(tn, sc_tag[0])
            bc_tag[0] = pe.inc(tn.matmul(out=UP[:, 0:8], lhsT=onesf[0:1, :],
                                         rhs=sval[0:1, 0:8], start=True, stop=True))
        blk.tensor(_bc)

        bvec_tag = [None]

        def _bv(v):
            _w(v, bc_tag[0])
            v.tensor_copy(out=bvec[:], in_=UP[:, 0:8])
            v.drain()
            v.drain()
            v.tensor_scalar(out=b1s[:], in0=b12[:], scalar1=bvec[:, 5:6], scalar2=None,
                            op0=OP.mult)
            bvec_tag[0] = vfence(v)
        blk.vector(_bv)

        if STOP == "cc1":
            barrier()
            finish(fdumps=(bvec,))
            return nc

        w1_free = [None, None]
        ln2ff = [ln2f_free[0], ln2f_free[1]]
        for dc in range(8):
            sl = dc % 2
            lt = [None]

            def _l(e, sl=sl, dc=dc, lt=lt):
                _w(e, w1_free[sl])
                _w(e, s_w[sl].tag())
                lt[0] = s_w[sl].inc(
                    e.dma_start(out=w1st[sl][:], in_=w1t_h[128 * dc:128 * (dc + 1), :]), 16)
            blk.sync(_l)

            for qq in range(4):
                sl2 = (dc * 4 + qq) % 2
                aq = [None]

                def _q1(s, sl=sl, sl2=sl2, qq=qq, lt=lt, aq=aq):
                    _w(s, lt[0], bvec_tag[0], ln2ff[sl2])
                    aq[0] = se.inc(s.activation(out=ln2f[sl2][:],
                                                in_=w1st[sl][:, 1024 * qq:1024 * (qq + 1)],
                                                func=AF.Identity, bias=0.0,
                                                scale=bvec[:, 6:7]))
                blk.scalar(_q1)

                def _q2(v, sl2=sl2, dc=dc, qq=qq, aq=aq):
                    _w(v, aq[0])
                    v.tensor_scalar(
                        out=w1qT[:, 4096 * dc + 1024 * qq:4096 * dc + 1024 * (qq + 1)],
                        in0=ln2f[sl2][:], scalar1=MAGIC, scalar2=-MAGIC,
                        op0=OP.add, op1=OP.add)
                    if dc == 7 and qq == 3:
                        vfence(v)
                    else:
                        vcar(v)
                blk.vector(_q2)
                ln2ff[sl2] = ve.tag()
            w1_free[sl] = ve.tag()
        w1q_done = ve.tag()
        ln2f_free = [w1q_done, w1q_done]

        if STOP == "w1q":
            barrier()
            finish(dumps=(w1qT[:, 0:16384],))
            return nc

        xqv_tag = [None]

        def _xqv(v):
            _w(v, bvec_tag[0])
            v.tensor_scalar(out=rstd2a[:], in0=rstd2b[:], scalar1=bvec[:, 4:5],
                            scalar2=None, op0=OP.mult)
            v.tensor_scalar(out=negm2a[:], in0=negm2b[:],
                            scalar1=bvec[:, 4:5], scalar2=None, op0=OP.mult)
            xqv_tag[0] = vfence(v)
        blk.vector(_xqv)

        xq_free = [None, None]
        for t in range(32):
            sl = t % 2
            lt = [None]

            def _l(e, sl=sl, t=t, lt=lt):
                _w(e, x1r_free[sl])
                _w(e, s_x1r[sl].tag())
                lt[0] = s_x1r[sl].inc(
                    e.dma_start(out=x1r[sl][:], in_=x1scr[128 * t:128 * (t + 1), :]), 16)
            blk.sync(_l)

            apt = [None]

            def _a(s, sl=sl, t=t, lt=lt, apt=apt):
                _w(s, lt[0], xqv_tag[0], ln2f_free[sl])
                s.activation(out=ln2f[sl][:], in_=x1r[sl][:], func=AF.Identity,
                             bias=negm2a[:, t:t + 1], scale=rstd2a[:, t:t + 1])
                apt[0] = sfence(s)
            blk.scalar(_a)
            x1r_free[sl] = se.tag()

            rq = [None]

            def _r(v, sl=sl, apt=apt, rq=rq):
                _w(v, apt[0], xq_free[sl])
                v.tensor_scalar(out=xqtl[sl][:], in0=ln2f[sl][:],
                                scalar1=MAGIC, scalar2=-MAGIC,
                                op0=OP.add, op1=OP.add)
                rq[0] = vfence(v)
            blk.vector(_r)
            ln2f_free[sl] = ve.tag()
            xq_free[sl] = transpose_tile(xqtl[sl], xqT, T, 128 * t, (rq[0],), (PF0, PF0b))
        xq_done = ve.tag()

        if STOP == "xq":
            barrier()
            finish(dumps=(xqT[:, 0:16384],))
            return nc

        mm_free = [None, None, None]
        htl_free = [None, None]
        it = 0
        for tt in range(8):
            for hc in range(32):
                psl = it % 3
                hsl = it % 2
                mt = [None]

                def _mm1(tn, tt=tt, hc=hc, psl=psl, mt=mt):
                    _w(tn, w1q_done if it == 0 else None, xq_done if it == 0 else None,
                       mm_free[psl])
                    for dc in range(8):
                        ii = tn.matmul(
                            out=MM[psl][:],
                            lhsT=w1qT[:, 4096 * dc + 128 * hc:4096 * dc + 128 * hc + 128],
                            rhs=xqT[:, T * dc + 512 * tt:T * dc + 512 * tt + 512],
                            start=(dc == 0), stop=(dc == 7))
                    mt[0] = pe.inc(ii)
                blk.tensor(_mm1)

                gt_ = [None]

                def _gel(s, hc=hc, psl=psl, hsl=hsl, mt=mt, gt_=gt_):
                    _w(s, mt[0])
                    if htl_free[hsl] is not None:
                        _w(s, *htl_free[hsl])
                    s.activation(out=htl[hsl][:], in_=MM[psl][:],
                                 func=AF.Gelu, bias=b1s[:, hc:hc + 1],
                                 scale=bvec[:, 5:6])
                    gt_[0] = sfence(s)
                blk.scalar(_gel)
                mm_free[psl] = se.tag()

                am_t = [None]

                def _am(v, hsl=hsl, it=it, gt_=gt_, am_t=am_t):
                    _w(v, gt_[0])
                    v.tensor_reduce(out=amhc[:, it:it + 1], in_=htl[hsl][:],
                                    axis=AX.X, op=OP.max,
                                    apply_absolute_value=True)
                    am_t[0] = vfence2(v)
                blk.vector(_am)

                def _spl(g, tt=tt, hc=hc, hsl=hsl, gt_=gt_):
                    _w(g, gt_[0], s_hw[hsl].tag())
                    s_hw[hsl].inc(g.dma_start(
                        out=hscr[128 * hc:128 * (hc + 1), 512 * tt:512 * (tt + 1)],
                        in_=htl[hsl][:]), 16)
                blk.gpsimd(_spl)
                htl_free[hsl] = (am_t[0], s_hw[hsl].tag())
                it += 1

        barrier(dma_tags=(s_hw[0].tag(), s_hw[1].tag()))
        if STOP == "mm1":
            finish(fdumps=(amh,))
            return nc
        fps.close()
        fes.close()

        f2ps = es.enter_context(ExitStack())
        GG = [f2ps.enter_context(nc.psum_tensor(f"GG{i}", [128, 1024], F32)) for i in range(3)]
        UP2 = f2ps.enter_context(nc.psum_tensor("UP2", [128, 128], F32))

        f2es = es.enter_context(ExitStack())
        w2qT = sb("w2qT", [128, 32 * 1024], BF16, f2es)
        b2r = sb("b2r_s", [1, D], F32, f2es)
        b2tag = [None]

        def _ldb2(e):
            _w(e, s_b.tag())
            b2tag[0] = s_b.inc(e.dma_start(out=b2r[:], in_=b2r_h[:, :]), 16)
        blk.sync(_ldb2)
        hrel = [sb(f"hrel{i}", [128, 512], F32, f2es) for i in range(2)]
        hqf = [sb(f"hqf{i}", [128, 512], F32, f2es) for i in range(2)]
        hqtl = [sb(f"hqtl{i}", [128, 512], BF16, f2es) for i in range(2)]
        x1r2 = x1r
        outt = [sb(f"outt{i}", [128, 512], F32, f2es) for i in range(4)]

        th = part_max_f2 = None

        def part_max2(src, dstcol, dep):
            tt_ = [None]

            def _t(tn, tt_=tt_, dep=dep):
                _w(tn, *dep)
                tt_[0] = pe.inc(tn.transpose(out=UP2[0:1, 0:128], in_=src, identity=idf[:]))
            blk.tensor(_t)

            def _r(v, tt_=tt_, dstcol=dstcol):
                _w(v, tt_[0])
                v.tensor_reduce(out=sval[0:1, dstcol:dstcol + 1], in_=UP2[0:1, 0:128],
                                axis=AX.X, op=OP.max)
                vfence2(v)
            blk.vector(_r)
            return ve.tag()

        def _hfin(v):
            v.drain()
            v.tensor_reduce(out=amh1[:], in_=amhc[:], axis=AX.X, op=OP.max)
            vfence2(v)
        blk.vector(_hfin)
        th = part_max2(amh1[:], 0, (ve.tag(),))
        cc2_tag = [None]

        def _prep_cc2b(v):
            _w(v, th)
            for j in range(8):
                v.tensor_copy(out=redt[0:1, j:j + 1], in_=sval[0:1, 0:1])
            cc2_tag[0] = vfence(v)
        blk.vector(_prep_cc2b)

        gm2_tag = [None]

        def _cc2(g):
            _w(g, cc2_tag[0])
            _w(g, s_cc.tag())
            s_cc.inc(g.dma_start(out=cci2[:, :], in_=redt[:]), 16)
            _w(g, s_cc.tag())
            ge.inc(g.collective_compute(
                "AllReduce", mybir.AluOpType.max,
                replica_groups=[list(range(NCORES))],
                ins=[cci2.ap().opt()], outs=[cco2.ap().opt()]))
            _w(g, ge.tag())
            _w(g, s_cc.tag())
            gm2_tag[0] = s_cc.inc(g.dma_start(out=gmt[:], in_=cco2[:, :]), 16)
        blk.gpsimd(_cc2)

        sc2_tag = [None]

        def _scl2(v):
            _w(v, gm2_tag[0])
            def SP():
                v.drain()
            v.tensor_reduce(out=svb[0:1, 0:1], in_=gmt[:], axis=AX.X, op=OP.max)
            _w(v, vfence2(v))
            v.tensor_scalar(out=svb[0:1, 3:4], in0=svb[0:1, 0:1], scalar1=1e-6,
                            scalar2=R127, op0=OP.max, op1=OP.mult)
            vcar(v, inc=False)
            SP()
            v.reciprocal(out=sval[0:1, 4:5], in_=svb[0:1, 3:4])
            v.tensor_tensor(out=sval[0:1, 5:6], in0=svb[0:1, 2:3],
                            in1=svb[0:1, 3:4], op=OP.mult)
            v.drain()
            sc2_tag[0] = ve.inc(v.tensor_copy(out=svb[0:1, 7:8], in_=sval[0:1, 4:5]))
        blk.vector(_scl2)

        bc2_tag = [None]

        def _bc2(tn):
            _w(tn, sc2_tag[0])
            bc2_tag[0] = pe.inc(tn.matmul(out=UP2[:, 0:8], lhsT=onesf[0:1, :],
                                          rhs=sval[0:1, 0:8], start=True, stop=True))
        blk.tensor(_bc2)

        bv2_tag = [None]

        def _bv2(v):
            _w(v, bc2_tag[0])
            v.tensor_copy(out=bvec[:], in_=UP2[:, 0:8])
            bv2_tag[0] = vfence(v)
        blk.vector(_bv2)

        if STOP == "cc2":
            barrier()
            finish(fdumps=(bvec,))
            return nc

        w2_free = [None, None]
        hq_stage_free = [None, None]
        for hc in range(32):
            sl = hc % 2
            lt = [None]

            def _l(e, sl=sl, hc=hc, lt=lt):
                _w(e, w2_free[sl])
                _w(e, s_w[sl].tag())
                lt[0] = s_w[sl].inc(
                    e.dma_start(out=w2st[sl][:], in_=w2t_h[128 * hc:128 * (hc + 1), :]), 16)
            blk.sync(_l)

            aq = [None]

            def _q1(s, sl=sl, lt=lt, aq=aq):
                _w(s, lt[0], bv2_tag[0], hq_stage_free[sl])
                aq[0] = se.inc(s.activation(out=ln2f[sl][:], in_=w2st[sl][:],
                             func=AF.Identity, bias=0.0, scale=bvec[:, 7:8]))
            blk.scalar(_q1)
            w2_free[sl] = se.tag()

            def _q2(v, sl=sl, hc=hc, aq=aq):
                _w(v, aq[0])
                v.tensor_scalar(out=w2qT[:, 1024 * hc:1024 * (hc + 1)],
                                in0=ln2f[sl][:], scalar1=MAGIC, scalar2=-MAGIC,
                                op0=OP.add, op1=OP.add)
                vcar(v) if hc < 31 else vfence(v)
            blk.vector(_q2)
            hq_stage_free[sl] = ve.tag()
        w2q_done = ve.tag()

        if STOP == "w2q":
            barrier()
            finish(dumps=(w2qT[:, 0:16384],))
            return nc

        groups = [(0, 3), (3, 3), (6, 3), (9, 3), (12, 3), (15, 3), (18, 3),
                  (21, 3), (24, 3), (27, 3), (30, 2)]
        hr_free = [None, None]
        hqf_free = [None, None]
        hq_free = [None, None]
        x1b_free = [None, None]
        out_free = [None, None, None, None]
        gg_free = [None, None, None]
        for gi, (t0, ntl) in enumerate(groups):
            mm_done = [None]
            for hc in range(32):
                sl = hc % 2
                lt = [None]

                def _l(e, sl=sl, hc=hc, t0=t0, ntl=ntl, lt=lt):
                    _w(e, hr_free[sl])
                    _w(e, s_hr[sl].tag())
                    lt[0] = s_hr[sl].inc(
                        e.dma_start(out=hrel[sl][:, 0:128 * ntl],
                                    in_=hscr[128 * hc:128 * (hc + 1),
                                             128 * t0:128 * (t0 + ntl)]), 16)
                blk.sync(_l)

                aq = [None]

                def _hq1(s, sl=sl, ntl=ntl, lt=lt, aq=aq):
                    _w(s, lt[0], hqf_free[sl])
                    aq[0] = se.inc(s.activation(out=hqf[sl][:, 0:128 * ntl],
                                                in_=hrel[sl][:, 0:128 * ntl],
                                                func=AF.Identity, bias=0.0,
                                                scale=bvec[:, 4:5]))
                blk.scalar(_hq1)
                hr_free[sl] = se.tag()

                rq = [None]

                def _hq2(v, sl=sl, ntl=ntl, aq=aq, rq=rq):
                    _w(v, aq[0], hq_free[sl])
                    v.tensor_scalar(out=hqtl[sl][:, 0:128 * ntl],
                                    in0=hqf[sl][:, 0:128 * ntl],
                                    scalar1=MAGIC, scalar2=-MAGIC,
                                    op0=OP.add, op1=OP.add)
                    rq[0] = vfence(v)
                blk.vector(_hq2)
                hqf_free[sl] = ve.tag()

                def _mm2(tn, sl=sl, hc=hc, ntl=ntl, rq=rq, gi=gi):
                    _w(tn, rq[0], w2q_done if gi == 0 and hc == 0 else None)
                    if hc == 0:
                        for j in range(ntl):
                            _w(tn, gg_free[j])
                    for j in range(ntl):
                        for hf in range(2):
                            ii = tn.matmul(
                                out=GG[j][:, 512 * hf:512 * hf + 512],
                                lhsT=hqtl[sl][:, 128 * j:128 * (j + 1)],
                                rhs=w2qT[:, 1024 * hc + 512 * hf:1024 * hc + 512 * hf + 512],
                                start=(hc == 0), stop=False)
                    pe.inc(ii)
                blk.tensor(_mm2)
                hq_free[sl] = pe.tag()
            mmb = [None]

            def _b2m(tn, ntl=ntl, mmb=mmb, gi=gi):
                _w(tn, b2tag[0] if gi == 0 else None)
                for j in range(ntl):
                    for hf in range(2):
                        ii = tn.matmul(out=GG[j][:, 512 * hf:512 * hf + 512],
                                       lhsT=onesf[0:1, :],
                                       rhs=b2r[0:1, 512 * hf:512 * hf + 512],
                                       start=False, stop=True)
                mmb[0] = pe.inc(ii)
            blk.tensor(_b2m)

            for j in range(ntl):
                t_ = t0 + j
                sl = j % 2
                lt = [None]

                def _lx(e, sl=sl, t_=t_, lt=lt):
                    _w(e, x1b_free[sl])
                    _w(e, s_x1r[sl].tag())
                    lt[0] = s_x1r[sl].inc(
                        e.dma_start(out=x1r2[sl][:], in_=x1scr[128 * t_:128 * (t_ + 1), :]), 16)
                blk.sync(_lx)

                et = [None]

                def _ev(v, j=j, sl=sl, lt=lt, mmb=mmb, et=et):
                    _w(v, mmb[0], lt[0])
                    for hf in range(2):
                        osl = (2 * j + hf) % 4
                        if out_free[osl] is not None:
                            _w(v, out_free[osl])
                        v.scalar_tensor_tensor(
                            out=outt[osl][:], in0=GG[j][:, 512 * hf:512 * hf + 512],
                            scalar=bvec[:, 5:6],
                            in1=x1r2[sl][:, 512 * hf:512 * hf + 512],
                            op0=OP.mult, op1=OP.add)
                    et[0] = vfence(v)
                blk.vector(_ev)

                def _so(s2, t_=t_, j=j, et=et):
                    _w(s2, et[0])
                    for hf in range(2):
                        osl = (2 * j + hf) % 4
                        _w(s2, s_out[osl].tag())
                        s_out[osl].inc(s2.dma_start(
                            out=out_h[128 * t_:128 * (t_ + 1), 512 * hf:512 * hf + 512],
                            in_=outt[osl][:]), 16)
                blk.scalar(_so)
                for hf in range(2):
                    osl = (2 * j + hf) % 4
                    out_free[osl] = s_out[osl].tag()
                x1b_free[sl] = ve.tag()
                gg_free[j] = et[0]

        def _fin(e):
            for q in s_out:
                _w(e, q.tag())
        blk.sync(_fin)
        f2ps.close()
        f2es.close()
        fesb.close()
    return nc


_CACHE = {}


def _get_nc():
    key = ("nc", STOP, DEBUG)
    if key not in _CACHE:
        _CACHE[key] = build()
    return _CACHE[key]


def _consts():
    if "consts" not in _CACHE:
        idb = np.eye(128, dtype=ml_dtypes.bfloat16)
        idf = np.eye(128, dtype=np.float32)
        kk = np.arange(128)[:, None]
        r = np.arange(128)[None, :]
        ut = (kk >= r).astype(np.float32)
        lt = (kk <= r).astype(np.float32)
        on = np.ones((128, 128), np.float32)
        m3 = np.concatenate([ut, on, lt], axis=1)
        maskt = np.concatenate([m3, m3], axis=1).astype(ml_dtypes.bfloat16)
        _CACHE["consts"] = (idb, idf, maskt)
    return _CACHE["consts"]


def kernel(L0_attn_x, L0_attn_Wq, L0_attn_bq, L0_attn_Wk, L0_attn_bk,
           L0_attn_Wv, L0_attn_bv, L0_attn_Wo, L0_attn_bo,
           L0_ffn_w1, L0_ffn_b1, L0_ffn_w2, L0_ffn_b2):
    nc = _get_nc()
    idb, idf, maskt = _consts()
    x = np.ascontiguousarray(np.asarray(L0_attn_x, dtype=np.float32))
    com = {
        "wq": np.ascontiguousarray(np.asarray(L0_attn_Wq, np.float32)),
        "wk": np.ascontiguousarray(np.asarray(L0_attn_Wk, np.float32)),
        "wv": np.ascontiguousarray(np.asarray(L0_attn_Wv, np.float32)),
        "wo": np.ascontiguousarray(np.asarray(L0_attn_Wo, np.float32)),
        "w1t": np.ascontiguousarray(np.asarray(L0_ffn_w1, np.float32).T),
        "w2t": np.ascontiguousarray(np.asarray(L0_ffn_w2, np.float32).T),
        "bq2": np.ascontiguousarray(np.asarray(L0_attn_bq, np.float32).reshape(8, 128).T),
        "bk2": np.ascontiguousarray(np.asarray(L0_attn_bk, np.float32).reshape(8, 128).T),
        "bv2": np.ascontiguousarray(np.asarray(L0_attn_bv, np.float32).reshape(8, 128).T),
        "bor": np.asarray(L0_attn_bo, np.float32).reshape(1, D),
        "b12": np.ascontiguousarray(np.asarray(L0_ffn_b1, np.float32).reshape(32, 128).T),
        "b2r": np.asarray(L0_ffn_b2, np.float32).reshape(1, D),
        "idb": idb, "idf": idf, "maskt": maskt,
    }
    in_maps = []
    for c in range(NCORES):
        m = dict(com)
        m["x"] = np.ascontiguousarray(x[2 * c:2 * c + 2].reshape(T, D))
        in_maps.append(m)
    res = run_bass_kernel_spmd(nc, in_maps, core_ids=list(range(NCORES)))
    out = np.empty((16, S, D), np.float32)
    for c in range(NCORES):
        out[2 * c:2 * c + 2] = np.asarray(res.results[c]["out"]).reshape(2, S, D)
    return out


# version 06
# speedup vs baseline: 1.0748x; 1.0748x over previous
import numpy as np
from contextlib import ExitStack
import ml_dtypes
import concourse.bass as bass
import concourse.mybir as mybir
from concourse.bass_utils import run_bass_kernel_spmd
from concourse.alu_op_type import AluOpType as OP

F32 = mybir.dt.float32
BF16 = mybir.dt.bfloat16
AF = mybir.ActivationFunctionType
AX = mybir.AxisListType

MAGIC = float(np.float32(1.5 * 2 ** 23))
RD = float(np.float32(1.0) / np.float32(1024.0))
ESHIFT = -12.0
R127 = float(np.float32(1.0) / np.float32(127.0))
S, D, T, HID = 2048, 1024, 4096, 4096
NCORES = 8
STOP = "all"
DEBUG = False


class Ctr:
    __slots__ = ("h", "n")

    def __init__(self, h):
        self.h = h
        self.n = 0

    def inc(self, inst, k=1):
        inst.then_inc(self.h, k)
        self.n += k
        return (self, self.n)

    def tag(self):
        return (self, self.n)


def _w(eng, *tags):
    for t in tags:
        if t is not None and t[1] > 0:
            eng.wait_ge(t[0].h, t[1])


def build():
    nc = bass.Bass("TRN2", target_bir_lowering=False)

    x_h = nc.dram_tensor("x", [T, D], F32, kind="ExternalInput").ap()
    wq_h = nc.dram_tensor("wq", [D, D], F32, kind="ExternalInput").ap()
    wk_h = nc.dram_tensor("wk", [D, D], F32, kind="ExternalInput").ap()
    wv_h = nc.dram_tensor("wv", [D, D], F32, kind="ExternalInput").ap()
    wo_h = nc.dram_tensor("wo", [D, D], F32, kind="ExternalInput").ap()
    w1t_h = nc.dram_tensor("w1t", [D, HID], F32, kind="ExternalInput").ap()
    w2t_h = nc.dram_tensor("w2t", [HID, D], F32, kind="ExternalInput").ap()
    bq2_h = nc.dram_tensor("bq2", [128, 8], F32, kind="ExternalInput").ap()
    bk2_h = nc.dram_tensor("bk2", [128, 8], F32, kind="ExternalInput").ap()
    bv2_h = nc.dram_tensor("bv2", [128, 8], F32, kind="ExternalInput").ap()
    bor_h = nc.dram_tensor("bor", [1, D], F32, kind="ExternalInput").ap()
    b12_h = nc.dram_tensor("b12", [128, 32], F32, kind="ExternalInput").ap()
    b2r_h = nc.dram_tensor("b2r", [1, D], F32, kind="ExternalInput").ap()
    idb_h = nc.dram_tensor("idb", [128, 128], BF16, kind="ExternalInput").ap()
    idf_h = nc.dram_tensor("idf", [128, 128], F32, kind="ExternalInput").ap()
    mk_h = nc.dram_tensor("maskt", [128, 768], BF16, kind="ExternalInput").ap()
    out_h = nc.dram_tensor("out", [T, D], F32, kind="ExternalOutput").ap()
    if DEBUG:
        dbg_h = nc.dram_tensor("dbg", [128, 49408], BF16, kind="ExternalOutput").ap()
        dbgx_h = nc.dram_tensor("dbgx", [T, D], BF16, kind="ExternalOutput").ap()
        dbgf_h = nc.dram_tensor("dbgf", [128, 64], F32, kind="ExternalOutput").ap()
    hscr = nc.dram_tensor("hscr", [HID, T], F32)
    x1scr = nc.dram_tensor("x1scr", [T, D], BF16)
    cci1 = nc.dram_tensor("cci1", [1, 8], F32)
    cco1 = nc.dram_tensor("cco1", [1, 8], F32)
    cci2 = nc.dram_tensor("cci2", [1, 8], F32)
    cco2 = nc.dram_tensor("cco2", [1, 8], F32)

    es = ExitStack()
    with es:
        def sb(name, shape, dt, st=es):
            return st.enter_context(nc.sbuf_tensor(name, shape, dt))

        idb = sb("idb_s", [128, 128], BF16)
        idf = sb("idf_s", [128, 128], F32)
        maskt = sb("maskt_s", [128, 768], BF16)
        bq2 = sb("bq2_s", [128, 8], F32)
        bk2 = sb("bk2_s", [128, 8], F32)
        bv2 = sb("bv2_s", [128, 8], F32)
        bor = sb("bor_s", [1, D], F32)
        b12 = sb("b12_s", [128, 32], F32)
        b1s = sb("b1s", [128, 32], F32)
        b2r = sb("b2r_s", [1, D], F32)
        bop = sb("boprime", [1, D], F32)
        onesf = sb("onesf", [1, 128], F32)
        esh = sb("esh", [128, 1], F32)
        rstd2a = sb("rstd2a", [128, 32], F32)
        negm2a = sb("negm2a", [128, 32], F32)
        redt = sb("redt", [1, 8], F32)
        gmt = sb("gmt", [1, 8], F32)
        sval = sb("sval", [1, 8], F32)
        bvec = sb("bvec", [128, 8], F32)
        lnv = [sb(f"lnv{i}", [128, 8], F32) for i in range(2)]
        svb = sb("svb", [1, 8], F32)
        rstd2b = sb("rstd2b", [128, 32], F32)
        negm2b = sb("negm2b", [128, 32], F32)
        spc = sb("spc", [128, 256], BF16)
        sA1 = sb("sA1", [128, 16], F32)
        qA1 = sb("qA1", [128, 16], F32)
        mA1 = sb("mA1", [128, 16], F32)
        eA1 = sb("eA1", [128, 16], F32)
        m2A1 = sb("m2A1", [128, 16], F32)
        veA1 = sb("veA1", [128, 16], F32)
        rvA1 = sb("rvA1", [128, 16], F32)
        rstdA1 = sb("rstdA1", [128, 16], F32)
        negmA1 = sb("negmA1", [128, 16], F32)
        sA2 = sb("sA2", [128, 32], F32)
        qA2 = sb("qA2", [128, 32], F32)
        mA2 = sb("mA2", [128, 32], F32)
        eA2 = sb("eA2", [128, 32], F32)
        m2A2 = sb("m2A2", [128, 32], F32)
        veA2 = sb("veA2", [128, 32], F32)
        rvA2 = sb("rvA2", [128, 32], F32)
        amxc = sb("amxc", [128, 32], F32)
        amhc = sb("amhc", [128, 256], F32)
        aw1c = sb("aw1c", [128, 8], F32)
        aw2c = sb("aw2c", [128, 32], F32)
        amx1 = sb("amx1", [128, 1], F32)
        amh1 = sb("amh1", [128, 1], F32)
        aw11 = sb("aw11", [128, 1], F32)
        aw21 = sb("aw21", [128, 1], F32)

        def mk(name):
            return Ctr(es.enter_context(nc.semaphore(name)))

        pe, ve, se, ge = mk("pe"), mk("ve"), mk("se"), mk("ge")
        s_x = [mk("s_x0"), mk("s_x1")]
        s_w = [mk("s_w0"), mk("s_w1")]
        s_b = mk("s_b")
        s_x1w = [mk("s_x1w0"), mk("s_x1w1")]
        s_x1r = [mk("s_x1r0"), mk("s_x1r1")]
        s_hw = [mk(f"s_hw{i}") for i in range(3)]
        s_hr = [mk(f"s_hr{i}") for i in range(4)]
        s_cc = mk("s_cc")
        s_out = [mk(f"s_out{i}") for i in range(4)]

        blk = es.enter_context(nc.Block())
        ENGS = {"t": blk.tensor, "v": blk.vector, "s": blk.scalar, "g": blk.gpsimd}
        ECTR = {"t": pe, "v": ve, "s": se, "g": ge}

        def barrier(dma_tags=()):
            tags = []
            for k in ("t", "v", "s", "g"):
                c = ECTR[k]

                def _f(e, c=c):
                    e.drain()
                    c.inc(e.nop())
                ENGS[k](_f)
                tags.append(c.tag())

            def _fy(e, tags=tags, dma_tags=tuple(dma_tags)):
                _w(e, *tags)
                for dt_ in dma_tags:
                    _w(e, dt_)
            for k in ("t", "v", "s", "g"):
                ENGS[k](_fy)
            blk.sync(_fy)

        def finish(dumps=(), hbm_dumps=(), fdumps=()):
            def _d(e):
                if DEBUG:
                    off = 0
                    for srct in dumps:
                        w = srct.shape[1]
                        s_out[0].inc(e.dma_start(out=dbg_h[:, off:off + w], in_=srct[:]), 16)
                        off += w
                    for dst, srcafé in hbm_dumps:
                        s_out[0].inc(e.dma_start(out=dst[:, :], in_=srcafé[:, :]), 16)
                    off = 0
                    for srct in fdumps:
                        w = srct.shape[1]
                        s_out[0].inc(e.dma_start(out=dbgf_h[:, off:off + w], in_=srct[:]), 16)
                        off += w
                _w(e, s_out[0].tag())
            blk.sync(_d)

        scar = sb("scar", [128, 1], F32)

        def sfence(s):
            s.drain()
            return se.inc(s.activation(out=scar[:], in_=esh[:], func=AF.Identity))

        def vfence2(v):
            vfence(v)
            return vfence(v)

        _vfk = [0]

        def vcar(v, inc=True):
            o = 8 * (_vfk[0] % 16)
            _vfk[0] += 1
            ii = v.tensor_copy(out=spc[:, o:o + 8], in_=maskt[:, o:o + 8])
            if inc:
                return ve.inc(ii)
            return None

        def vfence(v):
            v.drain()
            return vcar(v)

        def _pre_y(e):
            for dst, src in ((idb, idb_h), (idf, idf_h), (maskt, mk_h),
                             (bq2, bq2_h), (bk2, bk2_h), (bv2, bv2_h),
                             (bor, bor_h), (b12, b12_h), (b2r, b2r_h)):
                _w(e, s_b.tag())
                s_b.inc(e.dma_start(out=dst[:], in_=src[:, :]), 16)
        blk.sync(_pre_y)

        def _pre_v(v):
            _w(v, s_b.tag())
            v.memset(onesf[:], 1.0)
            v.memset(esh[:], ESHIFT)
            ve.inc(v.memset(spc[:], 0.0))
        blk.vector(_pre_v)

        aes = es.enter_context(ExitStack())
        Qs = sb("Qs", [128, 8 * S], BF16, aes)
        Ks = sb("Ks", [128, 8 * S], BF16, aes)
        Vs = sb("Vs", [128, 16 * 1040], BF16, aes)
        ln1T = sb("ln1T", [128, 8 * S], BF16, aes)
        AOs = ln1T
        wstage = [sb(f"wstg{i}", [128, 512], F32, aes) for i in range(2)]
        wb = sb("wb", [128, 8 * 1024], BF16, aes)
        xin = [sb(f"xin{i}", [128, 1024], F32, aes) for i in range(2)]
        lntl = [sb(f"lntl{i}", [128, 1024], BF16, aes) for i in range(2)]
        ptsb = [sb(f"ptsb{i}", [128, 768], BF16, aes) for i in range(2)]
        qz = [sb(f"qz{i}", [128, 2 * S], BF16, aes) for i in range(2)]
        ptm = [sb(f"ptm{i}", [128, 768], BF16, aes) for i in range(2)]
        rrt = [sb(f"rrt{i}", [128, 2], F32, aes) for i in range(2)]
        x1t = [sb(f"x1t{i}", [128, 1024], BF16, aes) for i in range(2)]

        aps = es.enter_context(ExitStack())
        P0 = aps.enter_context(nc.psum_tensor("P0", [128, 512], BF16))
        P0b = aps.enter_context(nc.psum_tensor("P0b", [128, 512], BF16))
        SC = [aps.enter_context(nc.psum_tensor("SC0", [128, 1024], F32)),
              aps.enter_context(nc.psum_tensor("SC1", [128, 1024], F32))]
        AT = [aps.enter_context(nc.psum_tensor("AT0", [128, 512], F32)),
              aps.enter_context(nc.psum_tensor("AT1", [128, 512], F32))]

        def _vones(v):
            ve.inc(v.memset(Vs[:, 64::65], 1.0))
        blk.vector(_vones)
        barrier(dma_tags=(s_b.tag(),))

        if STOP == "pre":
            barrier()
            finish()
            return nc

        tp_state = {"free": [None, None]}

        def transpose_tile(src, dstT, W, toff, dep_tags, pbanks):
            last = None
            for hh in range(2):
                ps = pbanks[hh]
                t_half = [None]

                def _tr(tn, ps=ps, hh=hh, t_half=t_half, dep_tags=tuple(dep_tags)):
                    _w(tn, *dep_tags, tp_state["free"][hh])
                    for j in range(4):
                        ii = tn.transpose(out=ps[:, 128 * j:128 * (j + 1)],
                                          in_=src[:, 128 * (4 * hh + j):128 * (4 * hh + j + 1)],
                                          identity=idb[:])
                    t_half[0] = pe.inc(ii)
                blk.tensor(_tr)

                def _ev(v, ps=ps, hh=hh, t_half=t_half):
                    _w(v, t_half[0])
                    v.tensor_copy(
                        out=dstT[:].rearrange("p (c n) -> p c n", n=W)[
                            :, 4 * hh:4 * hh + 4, toff:toff + 128],
                        in_=ps[:].rearrange("p (c n) -> p c n", n=128))
                    vfence(v)
                blk.vector(_ev)
                tp_state["free"][hh] = ve.tag()
                last = ve.tag()
            return last

        xin_free = [None, None]
        xin_free2 = [None, None]
        lnt_free = [None, None]
        at_free = [None, None]
        x1_free = [None, None]
        wb_free = None

        for sq in range(2):
            tok0 = sq * S
            for t in range(16):
                sl = t % 2
                row = tok0 + 128 * t
                ld_tag = [None]

                def _ld(e, sl=sl, row=row, ld_tag=ld_tag):
                    _w(e, xin_free[sl], xin_free2[sl])
                    _w(e, s_x[sl].tag())
                    ld_tag[0] = s_x[sl].inc(
                        e.dma_start(out=xin[sl][:], in_=x_h[row:row + 128, :]), 16)
                blk.sync(_ld)

                def _rs(v, sl=sl, t=t, ld_tag=ld_tag):
                    _w(v, ld_tag[0])
                    v.tensor_reduce(out=sA1[:, t:t + 1], in_=xin[sl][:],
                                    axis=AX.X, op=OP.add)
                    vfence2(v)
                blk.vector(_rs)
                xin_free[sl] = ve.tag()

                def _qs(s, sl=sl, t=t, ld_tag=ld_tag):
                    _w(s, ld_tag[0])
                    s.activation(out=lntl[sl][:], in_=xin[sl][:], func=AF.Square,
                                 accum_out=qA1[:, t:t + 1])
                    sfence(s)
                blk.scalar(_qs)
                xin_free2[sl] = se.tag()

            bm_tag = [None]

            def _bm(v):
                _w(v, xin_free[0], xin_free[1], xin_free2[0], xin_free2[1])
                v.tensor_scalar(out=mA1[:], in0=sA1[:], scalar1=RD, scalar2=None, op0=OP.mult)
                v.tensor_scalar(out=eA1[:], in0=qA1[:], scalar1=RD, scalar2=1e-5,
                                op0=OP.mult, op1=OP.add)
                v.drain()
                v.tensor_tensor(out=m2A1[:], in0=mA1[:], in1=mA1[:], op=OP.mult)
                v.drain()
                v.scalar_tensor_tensor(out=veA1[:], in0=m2A1[:], scalar=-1.0, in1=eA1[:],
                                       op0=OP.mult, op1=OP.add)
                v.drain()
                v.reciprocal(out=rvA1[:], in_=veA1[:])
                bm_tag[0] = vfence(v)
            blk.vector(_bm)

            sq_tag = [None]

            def _bs(s):
                _w(s, bm_tag[0], xin_free2[0], xin_free2[1])
                s.activation(out=rstdA1[:], in_=rvA1[:], func=AF.Sqrt)
                sq_tag[0] = sfence(s)
            blk.scalar(_bs)

            ng_tag = [None]

            def _bn2(v):
                _w(v, sq_tag[0])
                v.scalar_tensor_tensor(out=negmA1[:], in0=mA1[:], scalar=-1.0, in1=rstdA1[:],
                                       op0=OP.mult, op1=OP.mult)
                ng_tag[0] = vfence(v)
            blk.vector(_bn2)

            for t in range(16):
                sl = t % 2
                row = tok0 + 128 * t
                ld_tag = [None]

                def _ld2(e, sl=sl, row=row, ld_tag=ld_tag):
                    _w(e, xin_free[sl], xin_free2[sl])
                    _w(e, s_x[sl].tag())
                    ld_tag[0] = s_x[sl].inc(
                        e.dma_start(out=xin[sl][:], in_=x_h[row:row + 128, :]), 16)
                blk.sync(_ld2)

                ap_tag = [None]

                def _ap(s, sl=sl, t=t, ld_tag=ld_tag, ap_tag=ap_tag):
                    _w(s, ld_tag[0], ng_tag[0], lnt_free[sl])
                    s.activation(out=lntl[sl][:], in_=xin[sl][:],
                                 func=AF.Identity,
                                 bias=negmA1[:, t:t + 1],
                                 scale=rstdA1[:, t:t + 1])
                    ap_tag[0] = sfence(s)
                blk.scalar(_ap)
                xin_free[sl] = se.tag()
                lnt_free[sl] = transpose_tile(lntl[sl], ln1T, S, 128 * t,
                                              (ap_tag[0],), (P0, P0b))

            barrier()
            if STOP == "A0":
                finish(dumps=(ln1T,))
                return nc

            def load_w(src_h, dst, wb_free):
                wst_free = [None, None]
                last = [None]
                for ch in range(16):
                    dc, hf = ch // 2, ch % 2
                    sl = ch % 2
                    lt = [None]

                    def _lw(e, sl=sl, dc=dc, hf=hf, lt=lt, src_h=src_h):
                        _w(e, wst_free[sl])
                        _w(e, s_w[sl].tag())
                        lt[0] = s_w[sl].inc(
                            e.dma_start(out=wstage[sl][:],
                                        in_=src_h[128 * dc:128 * (dc + 1),
                                                  512 * hf:512 * (hf + 1)]), 16)
                    blk.sync(_lw)

                    def _cw(v, sl=sl, dc=dc, hf=hf, ch=ch, lt=lt, dst=dst,
                            wb_free=wb_free, last=last):
                        _w(v, lt[0], wb_free if ch == 0 else None)
                        v.tensor_copy(out=dst[:, 1024 * dc + 512 * hf:1024 * dc + 512 * hf + 512],
                                      in_=wstage[sl][:])
                        if ch % 2 == 1 or ch == 15:
                            last[0] = vfence(v)
                    blk.vector(_cw)
                    if ch % 2 == 1 or ch == 15:
                        wst_free[0] = ve.tag()
                        wst_free[1] = ve.tag()
                return last[0]

            def proj_featmajor(dst, bias2d, wtag):
                nonlocal wb_free
                it = 0
                for hp in range(8):
                    for tq in range(4):
                        sl = it % 2
                        mt = [None]

                        def _mm(tn, hp=hp, tq=tq, sl=sl, mt=mt, wtag=wtag):
                            _w(tn, wtag, at_free[sl])
                            for dc in range(8):
                                ii = tn.matmul(
                                    out=AT[sl][:],
                                    lhsT=wb[:, 1024 * dc + 128 * hp:1024 * dc + 128 * hp + 128],
                                    rhs=ln1T[:, S * dc + 512 * tq:S * dc + 512 * tq + 512],
                                    start=(dc == 0), stop=(dc == 7))
                            mt[0] = pe.inc(ii)
                        blk.tensor(_mm)

                        def _evq(s, hp=hp, tq=tq, sl=sl, mt=mt, dst=dst, bias2d=bias2d):
                            _w(s, mt[0])
                            if bias2d is not None:
                                s.activation(
                                    out=dst[:, S * hp + 512 * tq:S * hp + 512 * tq + 512],
                                    in_=AT[sl][:], func=AF.Identity,
                                    bias=bias2d[:, hp:hp + 1], scale=1.0)
                            else:
                                s.activation(
                                    out=dst[:, S * hp + 512 * tq:S * hp + 512 * tq + 512],
                                    in_=AT[sl][:], func=AF.Identity)
                            sfence(s)
                        blk.scalar(_evq)
                        at_free[sl] = se.tag()
                        it += 1
                wb_free = pe.tag()

            wtag = load_w(wq_h, wb, wb_free)
            proj_featmajor(Qs, bq2, wtag)
            barrier()
            if STOP == "Bq":
                barrier()
                finish(dumps=(Qs,))
                return nc
            wtag = load_w(wk_h, wb, wb_free)
            proj_featmajor(Ks, bk2, wtag)
            barrier()
            if STOP == "Bk":
                barrier()
                finish(dumps=(Qs, Ks))
                return nc
            wtag = load_w(wv_h, wb, wb_free)
            it = 0
            for tt in range(16):
                for hf in range(2):
                    sl = it % 2
                    mt = [None]

                    def _mmv(tn, tt=tt, hf=hf, sl=sl, mt=mt, wtag=wtag):
                        _w(tn, wtag, at_free[sl])
                        for dc in range(8):
                            ii = tn.matmul(
                                out=AT[sl][:],
                                lhsT=ln1T[:, S * dc + 128 * tt:S * dc + 128 * tt + 128],
                                rhs=wb[:, 1024 * dc + 512 * hf:1024 * dc + 512 * hf + 512],
                                start=(dc == 0), stop=(dc == 7))
                        mt[0] = pe.inc(ii)
                    blk.tensor(_mmv)

                    def _evv(s, tt=tt, hf=hf, sl=sl, mt=mt):
                        _w(s, mt[0])
                        s.activation(
                            out=Vs[:].rearrange("p (a b) -> p a b", b=65)[
                                :, 16 * tt + 8 * hf:16 * tt + 8 * hf + 8, 0:64],
                            in_=AT[sl][:].rearrange("p (a b) -> p a b", b=64),
                            func=AF.Identity)
                        sfence(s)
                    blk.scalar(_evv)
                    at_free[sl] = se.tag()
                    it += 1
            wb_free = pe.tag()
            barrier()

            wst_free2 = [None, None]
            bo_mm = [None]
            for dc in range(8):
                for hf in range(2):
                    sl = (2 * dc + hf) % 2
                    lt = [None]

                    def _lw(e, sl=sl, dc=dc, hf=hf, lt=lt):
                        for tg in (wst_free2[sl] or ()):
                            _w(e, tg)
                        _w(e, s_w[sl].tag())
                        lt[0] = s_w[sl].inc(
                            e.dma_start(out=wstage[sl][:],
                                        in_=wo_h[128 * dc:128 * (dc + 1),
                                                 512 * hf:512 * (hf + 1)]), 16)
                    blk.sync(_lw)

                    ct_ = [None]

                    def _cw(v, sl=sl, dc=dc, hf=hf, lt=lt, ct_=ct_):
                        _w(v, lt[0], wb_free if dc == 0 and hf == 0 else None)
                        v.tensor_copy(
                            out=wb[:, 1024 * dc + 512 * hf:1024 * dc + 512 * hf + 512],
                            in_=wstage[sl][:])
                        ct_[0] = vfence(v)
                    blk.vector(_cw)

                    if sq == 0:
                        def _bo(tn, sl=sl, dc=dc, hf=hf, lt=lt, bo_mm=bo_mm):
                            _w(tn, lt[0])
                            bo_mm[0] = pe.inc(tn.matmul(
                                out=SC[0][0:1, 512 * hf:512 * hf + 512],
                                lhsT=bv2[:, dc:dc + 1],
                                rhs=wstage[sl][:, 0:512], start=(dc == 0), stop=(dc == 7)))
                        blk.tensor(_bo)
                        wst_free2[sl] = (ct_[0], pe.tag())
                    else:
                        wst_free2[sl] = (ct_[0],)
            wotag = ve.tag()

            if sq == 0:
                def _boev(v, bo_mm=bo_mm):
                    _w(v, bo_mm[0])
                    v.tensor_tensor(out=bop[:], in0=SC[0][0:1, :], in1=bor[:],
                                    op=OP.add)
                    vfence(v)
                blk.vector(_boev)

            barrier()
            if STOP == "B0":
                finish(dumps=(Qs, Ks, Vs))
                return nc

            sc_free = [None, None]
            pt_free = [None, None]
            ptm_free = [None, None]
            atp_free = [None, None]
            prev_ev = [None]
            qz_free = [None, None]
            qz_tag = [None, None]
            it = 0
            for hp in range(8):
                qsl = hp % 2
                qztag = [None]

                def _qz(v, hp=hp, qsl=qsl, qztag=qztag):
                    _w(v, qz_free[qsl])
                    v.tensor_copy(out=qz[qsl][0:64, 0:S], in_=Qs[0:64, S * hp:S * (hp + 1)])
                    v.memset(qz[qsl][64:128, 0:S], 0.0)
                    v.tensor_copy(out=qz[qsl][64:128, S:2 * S],
                                  in_=Qs[64:128, S * hp:S * (hp + 1)])
                    v.memset(qz[qsl][0:64, S:2 * S], 0.0)
                    qztag[0] = vfence(v)
                blk.vector(_qz)
                qz_tag[qsl] = qztag[0]

                for qb in range(16):
                    sl = it % 2
                    if qb == 0:
                        tcs, moff = [0, 1], 128
                    elif qb == 15:
                        tcs, moff = [14, 15], 0
                    else:
                        tcs, moff = [qb - 1, qb, qb + 1], None
                    nch = len(tcs)
                    smt = [None]

                    def _sco(tn, hp=hp, qb=qb, sl=sl, qsl=qsl, tcs=tcs, smt=smt,
                             qztag=qztag):
                        _w(tn, sc_free[sl], qztag[0] if qb == 0 else None)
                        for hh in range(2):
                            for ci in range(len(tcs)):
                                tc = tcs[ci]
                                ii = tn.matmul(
                                    out=SC[sl][:, 384 * hh + 128 * ci:384 * hh + 128 * ci + 128],
                                    lhsT=Ks[:, S * hp + 128 * tc:S * hp + 128 * tc + 128],
                                    rhs=qz[qsl][:, S * hh + 128 * qb:S * hh + 128 * qb + 128],
                                    start=True, stop=True)
                        smt[0] = pe.inc(ii)
                    blk.tensor(_sco)

                    ext = [None]

                    def _exp(s, sl=sl, nch=nch, smt=smt, ext=ext):
                        _w(s, smt[0], pt_free[sl])
                        if nch == 3:
                            ii = s.activation(out=ptsb[sl][:], in_=SC[sl][:, 0:768],
                                              func=AF.Exp, bias=esh[:], scale=0.125)

                        else:
                            s.activation(out=ptsb[sl][:, 0:256], in_=SC[sl][:, 0:256],
                                         func=AF.Exp, bias=esh[:], scale=0.125)
                            ii = s.activation(out=ptsb[sl][:, 384:640], in_=SC[sl][:, 384:640],
                                              func=AF.Exp, bias=esh[:], scale=0.125)
                        ext[0] = sfence(s)
                    blk.scalar(_exp)
                    sc_free[sl] = se.tag()

                    mkt = [None]

                    def _msk(v, sl=sl, nch=nch, moff=moff, ext=ext, mkt=mkt):
                        _w(v, ext[0], ptm_free[sl])
                        if nch == 3:
                            v.tensor_tensor(out=ptm[sl][:], in0=ptsb[sl][:],
                                            in1=maskt[:], op=OP.mult)
                        else:
                            po = ptm[sl][:].rearrange("p (a b) -> p a b", b=384)[:, :, 0:256]
                            pv = ptsb[sl][:].rearrange("p (a b) -> p a b", b=384)[:, :, 0:256]
                            mv = maskt[:].rearrange("p (a b) -> p a b", b=384)[:, :, moff:moff + 256]
                            v.tensor_tensor(out=po, in0=pv, in1=mv, op=OP.mult)
                        v.drain()
                        mkt[0] = vcar(v)
                    blk.vector(_msk)

                    pvt = [None]

                    pt_free[sl] = ve.tag()

                    def _pv(tn, hp=hp, sl=sl, tcs=tcs, mkt=mkt, pvt=pvt):
                        _w(tn, mkt[0], atp_free[sl])
                        for hh in range(2):
                            h = 2 * hp + hh
                            for ci in range(len(tcs)):
                                tc = tcs[ci]
                                ii = tn.matmul(
                                    out=AT[sl][:, 65 * hh:65 * hh + 65],
                                    lhsT=ptm[sl][:, 384 * hh + 128 * ci:384 * hh + 128 * ci + 128],
                                    rhs=Vs[:, 1040 * tc + 65 * h:1040 * tc + 65 * h + 65],
                                    start=(ci == 0), stop=(ci == len(tcs) - 1))
                        pvt[0] = pe.inc(ii)
                    blk.tensor(_pv)
                    ptm_free[sl] = pe.tag()

                    def _aev(v, hp=hp, qb=qb, sl=sl, pvt=pvt):
                        _w(v, pvt[0])
                        v.reciprocal(out=rrt[sl][:], in_=AT[sl][:, 64:130:65])
                        pv = prev_ev[0]
                        if pv is not None:
                            php, pqb, psl = pv
                            v.tensor_scalar(
                                out=AOs[:, 1024 * pqb + 128 * php:1024 * pqb + 128 * php + 64],
                                in0=AT[psl][:, 0:64], scalar1=rrt[psl][:, 0:1],
                                scalar2=None, op0=OP.mult)
                            v.tensor_scalar(
                                out=AOs[:, 1024 * pqb + 128 * php + 64:1024 * pqb + 128 * php + 128],
                                in0=AT[psl][:, 65:129], scalar1=rrt[psl][:, 1:2],
                                scalar2=None, op0=OP.mult)
                            vfence(v)
                            atp_free[psl] = ve.tag()
                        else:
                            vcar(v)
                        prev_ev[0] = (hp, qb, sl)
                    blk.vector(_aev)
                    it += 1
                qz_free[qsl] = pe.tag()

            def _aev_fl(v):
                pv = prev_ev[0]
                php, pqb, psl = pv
                v.drain()
                v.tensor_scalar(
                    out=AOs[:, 1024 * pqb + 128 * php:1024 * pqb + 128 * php + 64],
                    in0=AT[psl][:, 0:64], scalar1=rrt[psl][:, 0:1],
                    scalar2=None, op0=OP.mult)
                v.tensor_scalar(
                    out=AOs[:, 1024 * pqb + 128 * php + 64:1024 * pqb + 128 * php + 128],
                    in0=AT[psl][:, 65:129], scalar1=rrt[psl][:, 1:2],
                    scalar2=None, op0=OP.mult)
                vfence(v)
                prev_ev[0] = None
            blk.vector(_aev_fl)

            barrier()
            if STOP == "C0":
                finish(dumps=(AOs,))
                return nc

            for t in range(16):
                sl = t % 2
                row = tok0 + 128 * t
                ld_tag = [None]

                def _ld(e, sl=sl, row=row, ld_tag=ld_tag):
                    _w(e, xin_free[sl])
                    _w(e, s_x[sl].tag())
                    ld_tag[0] = s_x[sl].inc(
                        e.dma_start(out=xin[sl][:], in_=x_h[row:row + 128, :]), 16)
                blk.sync(_ld)

                aot_tag = transpose_tile(AOs[:, 1024 * t:1024 * (t + 1)], lntl[sl], 128, 0,
                                         (lnt_free[sl],), (P0, P0b))
                om_last = None
                for hf in range(2):
                    psl = hf
                    mt = [None]

                    def _om(tn, hf=hf, sl=sl, psl=psl, mt=mt, aot_tag=aot_tag, t=t):
                        _w(tn, aot_tag, at_free[psl], wotag if t == 0 else None)
                        for dc in range(8):
                            tn.matmul(out=AT[psl][:],
                                      lhsT=lntl[sl][:, 128 * dc:128 * (dc + 1)],
                                      rhs=wb[:, 1024 * dc + 512 * hf:1024 * dc + 512 * hf + 512],
                                      start=(dc == 0), stop=False)
                        mt[0] = pe.inc(tn.matmul(
                            out=AT[psl][:], lhsT=onesf[0:1, :],
                            rhs=bop[0:1, 512 * hf:512 * hf + 512], start=False, stop=True))
                    blk.tensor(_om)

                    def _orv(v, hf=hf, sl=sl, psl=psl, mt=mt, ld_tag=ld_tag):
                        _w(v, mt[0], ld_tag[0], x1_free[sl] if hf == 0 else None)
                        v.tensor_tensor(
                            out=x1t[sl][:, 512 * hf:512 * hf + 512], in0=AT[psl][:],
                            in1=xin[sl][:, 512 * hf:512 * hf + 512], op=OP.add)
                        vfence(v)
                    blk.vector(_orv)
                    at_free[psl] = ve.tag()
                lnt_free[sl] = pe.tag()
                xin_free[sl] = ve.tag()

                ti = sq * 16 + t

                def _st2(v, sl=sl, ti=ti):
                    v.tensor_reduce(out=sA2[:, ti:ti + 1], in_=x1t[sl][:],
                                    axis=AX.X, op=OP.add)
                    vfence2(v)
                blk.vector(_st2)
                red_done = ve.tag()

                def _sq2(s, sl=sl, ti=ti, red_done=red_done):
                    _w(s, red_done)
                    s.activation(out=xin[sl][:], in_=x1t[sl][:], func=AF.Square,
                                 accum_out=qA2[:, ti:ti + 1])
                    sfence(s)
                blk.scalar(_sq2)
                stats_done = se.tag()
                xin_free[sl] = se.tag()

                def _sp(g, sl=sl, row=row, stats_done=stats_done):
                    _w(g, stats_done)
                    _w(g, s_x1w[sl].tag())
                    s_x1w[sl].inc(g.dma_start(out=x1scr[row:row + 128, :], in_=x1t[sl][:]), 16)
                blk.gpsimd(_sp)
                x1_free[sl] = s_x1w[sl].tag()

            barrier(dma_tags=(s_x1w[0].tag(), s_x1w[1].tag()))
            if STOP == "D0" and sq == 0:
                finish(hbm_dumps=((dbgx_h, x1scr),) if DEBUG else ())
                return nc

        if STOP == "attn":
            def _fatt(e):
                if DEBUG:
                    s_out[0].inc(e.dma_start(out=dbgx_h[:, :], in_=x1scr[:, :]), 16)
                _w(e, s_out[0].tag())
            blk.sync(_fatt)
            aps.close()
            aes.close()
            return nc

        aps.close()
        aes.close()

        fesb = es.enter_context(ExitStack())
        x1r = [sb(f"x1r{i}", [128, 1024], BF16, fesb) for i in range(2)]
        ln2f = [sb(f"ln2f{i}", [128, 1024], F32, fesb) for i in range(2)]
        w2st = [sb(f"w2st{i}", [128, 1024], F32, fesb) for i in range(2)]
        fes = es.enter_context(ExitStack())
        w1st = [sb(f"w1st{i}", [128, 4096], F32, fes) for i in range(2)]
        w1qT = sb("w1qT", [128, 8 * 4096], BF16, fes)
        xqT = sb("xqT", [128, 8 * 4096], BF16, fes)
        xqtl = [sb(f"xqtl{i}", [128, 1024], BF16, fes) for i in range(2)]
        htl = [sb(f"htl{i}", [128, 512], F32, fes) for i in range(2)]

        fps = es.enter_context(ExitStack())
        PF0 = fps.enter_context(nc.psum_tensor("PF0", [128, 512], BF16))
        PF0b = fps.enter_context(nc.psum_tensor("PF0b", [128, 512], BF16))
        MM = [fps.enter_context(nc.psum_tensor(f"MM{i}", [128, 512], F32)) for i in range(3)]
        UP = fps.enter_context(nc.psum_tensor("UP", [128, 128], F32))
        tp_state["free"] = [None, None]

        bm2_tag = [None]

        def _bm2(v):
            v.tensor_scalar(out=mA2[:], in0=sA2[:], scalar1=RD, scalar2=None, op0=OP.mult)
            v.tensor_scalar(out=eA2[:], in0=qA2[:], scalar1=RD, scalar2=1e-5,
                            op0=OP.mult, op1=OP.add)
            v.drain()
            v.tensor_tensor(out=m2A2[:], in0=mA2[:], in1=mA2[:], op=OP.mult)
            v.drain()
            v.scalar_tensor_tensor(out=veA2[:], in0=m2A2[:], scalar=-1.0, in1=eA2[:],
                                   op0=OP.mult, op1=OP.add)
            v.drain()
            v.reciprocal(out=rvA2[:], in_=veA2[:])
            bm2_tag[0] = vfence(v)
        blk.vector(_bm2)

        fin_tag = [None]

        def _fin_s(s):
            _w(s, bm2_tag[0])
            s.activation(out=rstd2b[:], in_=rvA2[:], func=AF.Sqrt)
            fin_tag[0] = sfence(s)
        blk.scalar(_fin_s)

        ln2fin = [None]

        def _fin_v(v):
            _w(v, fin_tag[0])
            v.scalar_tensor_tensor(out=negm2b[:], in0=mA2[:], scalar=-1.0, in1=rstd2b[:],
                                   op0=OP.mult, op1=OP.mult)
            ln2fin[0] = vfence(v)
        blk.vector(_fin_v)

        x1r_free = [None, None]
        ln2f_free = [None, None]
        for t in range(32):
            sl = t % 2
            lt = [None]

            def _l(e, sl=sl, t=t, lt=lt):
                _w(e, x1r_free[sl])
                _w(e, s_x1r[sl].tag())
                lt[0] = s_x1r[sl].inc(
                    e.dma_start(out=x1r[sl][:], in_=x1scr[128 * t:128 * (t + 1), :]), 16)
            blk.sync(_l)

            apt = [None]

            def _a(s, sl=sl, t=t, lt=lt, apt=apt):
                _w(s, lt[0], ln2fin[0], ln2f_free[sl])
                s.activation(out=ln2f[sl][:], in_=x1r[sl][:], func=AF.Identity,
                             bias=negm2b[:, t:t + 1], scale=rstd2b[:, t:t + 1])
                apt[0] = sfence(s)
            blk.scalar(_a)
            x1r_free[sl] = se.tag()

            def _m(v, sl=sl, t=t, apt=apt):
                _w(v, apt[0])
                v.tensor_reduce(out=amxc[:, t:t + 1], in_=ln2f[sl][:], axis=AX.X,
                                op=OP.max, apply_absolute_value=True)
                vfence(v)
            blk.vector(_m)
            ln2f_free[sl] = ve.tag()

        def _mfin(v):
            v.drain()
            v.tensor_reduce(out=amx1[:], in_=amxc[:], axis=AX.X, op=OP.max)
            vfence2(v)
        blk.vector(_mfin)
        amx_done = ve.tag()

        w1_free = [None, None]
        for dc in range(8):
            sl = dc % 2
            lt = [None]

            def _l(e, sl=sl, dc=dc, lt=lt):
                _w(e, w1_free[sl])
                _w(e, s_w[sl].tag())
                lt[0] = s_w[sl].inc(
                    e.dma_start(out=w1st[sl][:], in_=w1t_h[128 * dc:128 * (dc + 1), :]), 16)
            blk.sync(_l)

            def _m(v, sl=sl, dc=dc, lt=lt):
                _w(v, lt[0])
                v.tensor_reduce(out=aw1c[:, dc:dc + 1], in_=w1st[sl][:], axis=AX.X,
                                op=OP.max, apply_absolute_value=True)
                vfence(v)
            blk.vector(_m)
            w1_free[sl] = ve.tag()

        def _w1fin(v):
            v.drain()
            v.tensor_reduce(out=aw11[:], in_=aw1c[:], axis=AX.X, op=OP.max)
            vfence2(v)
        blk.vector(_w1fin)
        aw1_done = ve.tag()

        w2_free = [None, None]
        for hc in range(32):
            sl = hc % 2
            lt = [None]

            def _l(e, sl=sl, hc=hc, lt=lt):
                _w(e, w2_free[sl])
                _w(e, s_w[sl].tag())
                lt[0] = s_w[sl].inc(
                    e.dma_start(out=w2st[sl][:], in_=w2t_h[128 * hc:128 * (hc + 1), :]), 16)
            blk.sync(_l)

            def _m(v, sl=sl, hc=hc, lt=lt):
                _w(v, lt[0])
                v.tensor_reduce(out=aw2c[:, hc:hc + 1], in_=w2st[sl][:], axis=AX.X,
                                op=OP.max, apply_absolute_value=True)
                vfence(v)
            blk.vector(_m)
            w2_free[sl] = ve.tag()

        def _w2fin(v):
            v.drain()
            v.tensor_reduce(out=aw21[:], in_=aw2c[:], axis=AX.X, op=OP.max)
            vfence2(v)
        blk.vector(_w2fin)
        aw2_done = ve.tag()

        if STOP == "absmax":
            barrier()
            finish(fdumps=(amx, aw1, aw2))
            return nc

        def part_max(src, dstcol, dep):
            tt_ = [None]

            def _t(tn, tt_=tt_, dep=dep):
                _w(tn, *dep)
                tt_[0] = pe.inc(tn.transpose(out=UP[0:1, 0:128], in_=src, identity=idf[:]))
            blk.tensor(_t)

            def _r(v, tt_=tt_, dstcol=dstcol):
                _w(v, tt_[0])
                v.tensor_reduce(out=sval[0:1, dstcol:dstcol + 1], in_=UP[0:1, 0:128],
                                axis=AX.X, op=OP.max)
                vfence2(v)
            blk.vector(_r)
            return ve.tag()

        t1 = part_max(amx1[:], 0, (amx_done,))
        t2 = part_max(aw11[:], 1, (aw1_done, t1))
        t3 = part_max(aw21[:], 2, (aw2_done, t2))

        cc1_tag = [None]

        def _prep_cc2(v):
            _w(v, t3)
            for j in range(8):
                v.tensor_copy(out=redt[0:1, j:j + 1], in_=sval[0:1, 0:1])
            cc1_tag[0] = vfence(v)
        blk.vector(_prep_cc2)

        gm_tag = [None]

        def _cc1(g):
            _w(g, cc1_tag[0])
            _w(g, s_cc.tag())
            s_cc.inc(g.dma_start(out=cci1[:, :], in_=redt[:]), 16)
            _w(g, s_cc.tag())
            ge.inc(g.collective_compute(
                "AllReduce", mybir.AluOpType.max,
                replica_groups=[list(range(NCORES))],
                ins=[cci1.ap().opt()], outs=[cco1.ap().opt()]))
            _w(g, ge.tag())
            _w(g, s_cc.tag())
            gm_tag[0] = s_cc.inc(g.dma_start(out=gmt[:], in_=cco1[:, :]), 16)
        blk.gpsimd(_cc1)

        sc_tag = [None]

        def _scl(v):
            _w(v, gm_tag[0])
            def SP():
                v.drain()
            v.tensor_reduce(out=svb[0:1, 0:1], in_=gmt[:], axis=AX.X, op=OP.max)
            _w(v, vfence2(v))
            v.tensor_scalar(out=svb[0:1, 1:2], in0=sval[0:1, 1:2], scalar1=1e-6,
                            scalar2=R127, op0=OP.max, op1=OP.mult)
            v.tensor_scalar(out=svb[0:1, 2:3], in0=sval[0:1, 2:3], scalar1=1e-6,
                            scalar2=R127, op0=OP.max, op1=OP.mult)
            v.tensor_scalar(out=svb[0:1, 3:4], in0=svb[0:1, 0:1], scalar1=1e-6,
                            scalar2=R127, op0=OP.max, op1=OP.mult)
            _w(v, vfence(v))
            v.tensor_copy(out=sval[0:1, 3:4], in_=svb[0:1, 3:4])
            v.reciprocal(out=sval[0:1, 4:5], in_=svb[0:1, 3:4])
            v.tensor_tensor(out=sval[0:1, 5:6], in0=svb[0:1, 1:2], in1=svb[0:1, 3:4],
                            op=OP.mult)
            v.reciprocal(out=sval[0:1, 6:7], in_=svb[0:1, 1:2])
            v.reciprocal(out=sval[0:1, 7:8], in_=svb[0:1, 2:3])
            v.drain()
            sc_tag[0] = ve.inc(v.tensor_copy(out=svb[0:1, 7:8], in_=sval[0:1, 7:8]))
        blk.vector(_scl)

        bc_tag = [None]

        def _bc(tn):
            _w(tn, sc_tag[0])
            bc_tag[0] = pe.inc(tn.matmul(out=UP[:, 0:8], lhsT=onesf[0:1, :],
                                         rhs=sval[0:1, 0:8], start=True, stop=True))
        blk.tensor(_bc)

        bvec_tag = [None]

        def _bv(v):
            _w(v, bc_tag[0])
            v.tensor_copy(out=bvec[:], in_=UP[:, 0:8])
            v.drain()
            v.drain()
            v.tensor_scalar(out=b1s[:], in0=b12[:], scalar1=bvec[:, 5:6], scalar2=None,
                            op0=OP.mult)
            bvec_tag[0] = vfence(v)
        blk.vector(_bv)

        if STOP == "cc1":
            barrier()
            finish(fdumps=(bvec,))
            return nc

        w1_free = [None, None]
        ln2ff = [ln2f_free[0], ln2f_free[1]]
        for dc in range(8):
            sl = dc % 2
            lt = [None]

            def _l(e, sl=sl, dc=dc, lt=lt):
                _w(e, w1_free[sl])
                _w(e, s_w[sl].tag())
                lt[0] = s_w[sl].inc(
                    e.dma_start(out=w1st[sl][:], in_=w1t_h[128 * dc:128 * (dc + 1), :]), 16)
            blk.sync(_l)

            for qq in range(4):
                sl2 = (dc * 4 + qq) % 2
                aq = [None]

                def _q1(s, sl=sl, sl2=sl2, qq=qq, lt=lt, aq=aq):
                    _w(s, lt[0], bvec_tag[0], ln2ff[sl2])
                    s.activation(out=ln2f[sl2][:],
                                 in_=w1st[sl][:, 1024 * qq:1024 * (qq + 1)],
                                 func=AF.Identity, bias=0.0,
                                 scale=bvec[:, 6:7])
                    aq[0] = sfence(s)
                blk.scalar(_q1)

                def _q2(v, sl2=sl2, dc=dc, qq=qq, aq=aq):
                    _w(v, aq[0])
                    v.tensor_scalar(
                        out=w1qT[:, 4096 * dc + 1024 * qq:4096 * dc + 1024 * (qq + 1)],
                        in0=ln2f[sl2][:], scalar1=MAGIC, scalar2=-MAGIC,
                        op0=OP.add, op1=OP.add)
                    if dc == 7 and qq == 3:
                        vfence(v)
                    else:
                        vcar(v)
                blk.vector(_q2)
                ln2ff[sl2] = ve.tag()
            w1_free[sl] = ve.tag()
        w1q_done = ve.tag()
        ln2f_free = [w1q_done, w1q_done]

        if STOP == "w1q":
            barrier()
            finish(dumps=(w1qT[:, 0:16384],))
            return nc

        xqv_tag = [None]

        def _xqv(v):
            _w(v, bvec_tag[0])
            v.tensor_scalar(out=rstd2a[:], in0=rstd2b[:], scalar1=bvec[:, 4:5],
                            scalar2=None, op0=OP.mult)
            v.tensor_scalar(out=negm2a[:], in0=negm2b[:],
                            scalar1=bvec[:, 4:5], scalar2=None, op0=OP.mult)
            xqv_tag[0] = vfence(v)
        blk.vector(_xqv)

        xq_free = [None, None]
        for t in range(32):
            sl = t % 2
            lt = [None]

            def _l(e, sl=sl, t=t, lt=lt):
                _w(e, x1r_free[sl])
                _w(e, s_x1r[sl].tag())
                lt[0] = s_x1r[sl].inc(
                    e.dma_start(out=x1r[sl][:], in_=x1scr[128 * t:128 * (t + 1), :]), 16)
            blk.sync(_l)

            apt = [None]

            def _a(s, sl=sl, t=t, lt=lt, apt=apt):
                _w(s, lt[0], xqv_tag[0], ln2f_free[sl])
                s.activation(out=ln2f[sl][:], in_=x1r[sl][:], func=AF.Identity,
                             bias=negm2a[:, t:t + 1], scale=rstd2a[:, t:t + 1])
                apt[0] = sfence(s)
            blk.scalar(_a)
            x1r_free[sl] = se.tag()

            rq = [None]

            def _r(v, sl=sl, apt=apt, rq=rq):
                _w(v, apt[0], xq_free[sl])
                v.tensor_scalar(out=xqtl[sl][:], in0=ln2f[sl][:],
                                scalar1=MAGIC, scalar2=-MAGIC,
                                op0=OP.add, op1=OP.add)
                rq[0] = vfence(v)
            blk.vector(_r)
            ln2f_free[sl] = ve.tag()
            xq_free[sl] = transpose_tile(xqtl[sl], xqT, T, 128 * t, (rq[0],), (PF0, PF0b))
        xq_done = ve.tag()

        if STOP == "xq":
            barrier()
            finish(dumps=(xqT[:, 0:16384],))
            return nc

        mm_free = [None, None, None]
        htl_free = [None, None]
        it = 0
        for tt in range(8):
            for hc in range(32):
                psl = it % 3
                hsl = it % 2
                mt = [None]

                def _mm1(tn, tt=tt, hc=hc, psl=psl, mt=mt):
                    _w(tn, w1q_done if it == 0 else None, xq_done if it == 0 else None,
                       mm_free[psl])
                    for dc in range(8):
                        ii = tn.matmul(
                            out=MM[psl][:],
                            lhsT=w1qT[:, 4096 * dc + 128 * hc:4096 * dc + 128 * hc + 128],
                            rhs=xqT[:, T * dc + 512 * tt:T * dc + 512 * tt + 512],
                            start=(dc == 0), stop=(dc == 7))
                    mt[0] = pe.inc(ii)
                blk.tensor(_mm1)

                gt_ = [None]

                def _gel(s, hc=hc, psl=psl, hsl=hsl, mt=mt, gt_=gt_):
                    _w(s, mt[0])
                    if htl_free[hsl] is not None:
                        _w(s, *htl_free[hsl])
                    s.activation(out=htl[hsl][:], in_=MM[psl][:],
                                 func=AF.Gelu, bias=b1s[:, hc:hc + 1],
                                 scale=bvec[:, 5:6])
                    gt_[0] = sfence(s)
                blk.scalar(_gel)
                mm_free[psl] = se.tag()

                am_t = [None]

                def _am(v, hsl=hsl, it=it, gt_=gt_, am_t=am_t):
                    _w(v, gt_[0])
                    v.tensor_reduce(out=amhc[:, it:it + 1], in_=htl[hsl][:],
                                    axis=AX.X, op=OP.max,
                                    apply_absolute_value=True)
                    am_t[0] = vfence2(v)
                blk.vector(_am)

                def _spl(g, tt=tt, hc=hc, hsl=hsl, gt_=gt_):
                    _w(g, gt_[0], s_hw[hsl].tag())
                    s_hw[hsl].inc(g.dma_start(
                        out=hscr[128 * hc:128 * (hc + 1), 512 * tt:512 * (tt + 1)],
                        in_=htl[hsl][:]), 16)
                blk.gpsimd(_spl)
                htl_free[hsl] = (am_t[0], s_hw[hsl].tag())
                it += 1

        barrier(dma_tags=tuple(q.tag() for q in s_hw))
        if STOP == "mm1":
            finish(fdumps=(amh,))
            return nc
        fps.close()
        fes.close()

        f2ps = es.enter_context(ExitStack())
        GG = [f2ps.enter_context(nc.psum_tensor(f"GG{i}", [128, 1024], F32)) for i in range(3)]
        UP2 = f2ps.enter_context(nc.psum_tensor("UP2", [128, 128], F32))

        f2es = es.enter_context(ExitStack())
        w2qT = sb("w2qT", [128, 32 * 1024], BF16, f2es)
        hrel = [sb(f"hrel{i}", [128, 512], F32, f2es) for i in range(4)]
        hqf = [sb(f"hqf{i}", [128, 512], F32, f2es) for i in range(4)]
        hqtl = [sb(f"hqtl{i}", [128, 512], BF16, f2es) for i in range(4)]
        x1r2 = x1r
        outt = [sb(f"outt{i}", [128, 512], F32, f2es) for i in range(4)]

        th = part_max_f2 = None

        def part_max2(src, dstcol, dep):
            tt_ = [None]

            def _t(tn, tt_=tt_, dep=dep):
                _w(tn, *dep)
                tt_[0] = pe.inc(tn.transpose(out=UP2[0:1, 0:128], in_=src, identity=idf[:]))
            blk.tensor(_t)

            def _r(v, tt_=tt_, dstcol=dstcol):
                _w(v, tt_[0])
                v.tensor_reduce(out=sval[0:1, dstcol:dstcol + 1], in_=UP2[0:1, 0:128],
                                axis=AX.X, op=OP.max)
                vfence2(v)
            blk.vector(_r)
            return ve.tag()

        def _hfin(v):
            v.drain()
            v.tensor_reduce(out=amh1[:], in_=amhc[:], axis=AX.X, op=OP.max)
            vfence2(v)
        blk.vector(_hfin)
        th = part_max2(amh1[:], 0, (ve.tag(),))
        cc2_tag = [None]

        def _prep_cc2b(v):
            _w(v, th)
            for j in range(8):
                v.tensor_copy(out=redt[0:1, j:j + 1], in_=sval[0:1, 0:1])
            cc2_tag[0] = vfence(v)
        blk.vector(_prep_cc2b)

        gm2_tag = [None]

        def _cc2(g):
            _w(g, cc2_tag[0])
            _w(g, s_cc.tag())
            s_cc.inc(g.dma_start(out=cci2[:, :], in_=redt[:]), 16)
            _w(g, s_cc.tag())
            ge.inc(g.collective_compute(
                "AllReduce", mybir.AluOpType.max,
                replica_groups=[list(range(NCORES))],
                ins=[cci2.ap().opt()], outs=[cco2.ap().opt()]))
            _w(g, ge.tag())
            _w(g, s_cc.tag())
            gm2_tag[0] = s_cc.inc(g.dma_start(out=gmt[:], in_=cco2[:, :]), 16)
        blk.gpsimd(_cc2)

        sc2_tag = [None]

        def _scl2(v):
            _w(v, gm2_tag[0])
            def SP():
                v.drain()
            v.tensor_reduce(out=svb[0:1, 0:1], in_=gmt[:], axis=AX.X, op=OP.max)
            _w(v, vfence2(v))
            v.tensor_scalar(out=svb[0:1, 3:4], in0=svb[0:1, 0:1], scalar1=1e-6,
                            scalar2=R127, op0=OP.max, op1=OP.mult)
            vcar(v, inc=False)
            SP()
            v.reciprocal(out=sval[0:1, 4:5], in_=svb[0:1, 3:4])
            v.tensor_tensor(out=sval[0:1, 5:6], in0=svb[0:1, 2:3],
                            in1=svb[0:1, 3:4], op=OP.mult)
            v.drain()
            sc2_tag[0] = ve.inc(v.tensor_copy(out=svb[0:1, 7:8], in_=sval[0:1, 4:5]))
        blk.vector(_scl2)

        bc2_tag = [None]

        def _bc2(tn):
            _w(tn, sc2_tag[0])
            bc2_tag[0] = pe.inc(tn.matmul(out=UP2[:, 0:8], lhsT=onesf[0:1, :],
                                          rhs=sval[0:1, 0:8], start=True, stop=True))
        blk.tensor(_bc2)

        bv2_tag = [None]

        def _bv2(v):
            _w(v, bc2_tag[0])
            v.tensor_copy(out=bvec[:], in_=UP2[:, 0:8])
            bv2_tag[0] = vfence(v)
        blk.vector(_bv2)

        if STOP == "cc2":
            barrier()
            finish(fdumps=(bvec,))
            return nc

        w2_free = [None, None]
        hq_stage_free = [None, None]
        for hc in range(32):
            sl = hc % 2
            lt = [None]

            def _l(e, sl=sl, hc=hc, lt=lt):
                _w(e, w2_free[sl])
                _w(e, s_w[sl].tag())
                lt[0] = s_w[sl].inc(
                    e.dma_start(out=w2st[sl][:], in_=w2t_h[128 * hc:128 * (hc + 1), :]), 16)
            blk.sync(_l)

            aq = [None]

            def _q1(s, sl=sl, lt=lt, aq=aq):
                _w(s, lt[0], bv2_tag[0], hq_stage_free[sl])
                s.activation(out=ln2f[sl][:], in_=w2st[sl][:],
                             func=AF.Identity, bias=0.0, scale=bvec[:, 7:8])
                aq[0] = sfence(s)
            blk.scalar(_q1)
            w2_free[sl] = se.tag()

            def _q2(v, sl=sl, hc=hc, aq=aq):
                _w(v, aq[0])
                v.tensor_scalar(out=w2qT[:, 1024 * hc:1024 * (hc + 1)],
                                in0=ln2f[sl][:], scalar1=MAGIC, scalar2=-MAGIC,
                                op0=OP.add, op1=OP.add)
                vcar(v) if hc < 31 else vfence(v)
            blk.vector(_q2)
            hq_stage_free[sl] = ve.tag()
        w2q_done = ve.tag()

        if STOP == "w2q":
            barrier()
            finish(dumps=(w2qT[:, 0:16384],))
            return nc

        groups = [(0, 3), (3, 3), (6, 3), (9, 3), (12, 3), (15, 3), (18, 3),
                  (21, 3), (24, 3), (27, 3), (30, 2)]
        hr_free = [None] * 4
        hqf_free = [None] * 4
        hq_free = [None] * 4
        x1b_free = [None, None]
        out_free = [None, None, None, None]
        gg_free = [None, None, None]
        for gi, (t0, ntl) in enumerate(groups):
            mm_done = [None]
            for hc in range(32):
                sl = hc % 4
                lt = [None]

                def _l(e, sl=sl, hc=hc, t0=t0, ntl=ntl, lt=lt):
                    _w(e, hr_free[sl])
                    _w(e, s_hr[sl].tag())
                    lt[0] = s_hr[sl].inc(
                        e.dma_start(out=hrel[sl][:, 0:128 * ntl],
                                    in_=hscr[128 * hc:128 * (hc + 1),
                                             128 * t0:128 * (t0 + ntl)]), 16)
                blk.sync(_l)

                aq = [None]

                def _hq1(s, sl=sl, ntl=ntl, lt=lt, aq=aq):
                    _w(s, lt[0], hqf_free[sl])
                    s.activation(out=hqf[sl][:, 0:128 * ntl],
                                 in_=hrel[sl][:, 0:128 * ntl],
                                 func=AF.Identity, bias=0.0,
                                 scale=bvec[:, 4:5])
                    aq[0] = sfence(s)
                blk.scalar(_hq1)
                hr_free[sl] = se.tag()

                rq = [None]

                def _hq2(v, sl=sl, ntl=ntl, aq=aq, rq=rq):
                    _w(v, aq[0], hq_free[sl])
                    v.tensor_scalar(out=hqtl[sl][:, 0:128 * ntl],
                                    in0=hqf[sl][:, 0:128 * ntl],
                                    scalar1=MAGIC, scalar2=-MAGIC,
                                    op0=OP.add, op1=OP.add)
                    rq[0] = vfence(v)
                blk.vector(_hq2)
                hqf_free[sl] = ve.tag()

                def _mm2(tn, sl=sl, hc=hc, ntl=ntl, rq=rq, gi=gi):
                    _w(tn, rq[0], w2q_done if gi == 0 and hc == 0 else None)
                    if hc == 0:
                        for j in range(ntl):
                            _w(tn, gg_free[j])
                    for j in range(ntl):
                        for hf in range(2):
                            ii = tn.matmul(
                                out=GG[j][:, 512 * hf:512 * hf + 512],
                                lhsT=hqtl[sl][:, 128 * j:128 * (j + 1)],
                                rhs=w2qT[:, 1024 * hc + 512 * hf:1024 * hc + 512 * hf + 512],
                                start=(hc == 0), stop=False)
                    pe.inc(ii)
                blk.tensor(_mm2)
                hq_free[sl] = pe.tag()
            mmb = [None]

            def _b2m(tn, ntl=ntl, mmb=mmb):
                for j in range(ntl):
                    for hf in range(2):
                        ii = tn.matmul(out=GG[j][:, 512 * hf:512 * hf + 512],
                                       lhsT=onesf[0:1, :],
                                       rhs=b2r[0:1, 512 * hf:512 * hf + 512],
                                       start=False, stop=True)
                mmb[0] = pe.inc(ii)
            blk.tensor(_b2m)

            for j in range(ntl):
                t_ = t0 + j
                sl = j % 2
                lt = [None]

                def _lx(e, sl=sl, t_=t_, lt=lt):
                    _w(e, x1b_free[sl])
                    _w(e, s_x1r[sl].tag())
                    lt[0] = s_x1r[sl].inc(
                        e.dma_start(out=x1r2[sl][:], in_=x1scr[128 * t_:128 * (t_ + 1), :]), 16)
                blk.sync(_lx)

                et = [None]

                def _ev(v, j=j, sl=sl, lt=lt, mmb=mmb, et=et):
                    _w(v, mmb[0], lt[0])
                    for hf in range(2):
                        osl = (2 * j + hf) % 4
                        if out_free[osl] is not None:
                            _w(v, out_free[osl])
                        v.scalar_tensor_tensor(
                            out=outt[osl][:], in0=GG[j][:, 512 * hf:512 * hf + 512],
                            scalar=bvec[:, 5:6],
                            in1=x1r2[sl][:, 512 * hf:512 * hf + 512],
                            op0=OP.mult, op1=OP.add)
                    et[0] = vfence(v)
                blk.vector(_ev)

                def _so(s2, t_=t_, j=j, et=et):
                    _w(s2, et[0])
                    for hf in range(2):
                        osl = (2 * j + hf) % 4
                        _w(s2, s_out[osl].tag())
                        s_out[osl].inc(s2.dma_start(
                            out=out_h[128 * t_:128 * (t_ + 1), 512 * hf:512 * hf + 512],
                            in_=outt[osl][:]), 16)
                blk.scalar(_so)
                for hf in range(2):
                    osl = (2 * j + hf) % 4
                    out_free[osl] = s_out[osl].tag()
                x1b_free[sl] = ve.tag()
                gg_free[j] = et[0]

        def _fin(e):
            for q in s_out:
                _w(e, q.tag())
        blk.sync(_fin)
        f2ps.close()
        f2es.close()
        fesb.close()
    return nc


_CACHE = {}


def _get_nc():
    key = ("nc", STOP, DEBUG)
    if key not in _CACHE:
        _CACHE[key] = build()
    return _CACHE[key]


def _consts():
    if "consts" not in _CACHE:
        idb = np.eye(128, dtype=ml_dtypes.bfloat16)
        idf = np.eye(128, dtype=np.float32)
        kk = np.arange(128)[:, None]
        r = np.arange(128)[None, :]
        ut = (kk >= r).astype(np.float32)
        lt = (kk <= r).astype(np.float32)
        on = np.ones((128, 128), np.float32)
        m3 = np.concatenate([ut, on, lt], axis=1)
        maskt = np.concatenate([m3, m3], axis=1).astype(ml_dtypes.bfloat16)
        _CACHE["consts"] = (idb, idf, maskt)
    return _CACHE["consts"]


def kernel(L0_attn_x, L0_attn_Wq, L0_attn_bq, L0_attn_Wk, L0_attn_bk,
           L0_attn_Wv, L0_attn_bv, L0_attn_Wo, L0_attn_bo,
           L0_ffn_w1, L0_ffn_b1, L0_ffn_w2, L0_ffn_b2):
    nc = _get_nc()
    idb, idf, maskt = _consts()
    x = np.ascontiguousarray(np.asarray(L0_attn_x, dtype=np.float32))
    com = {
        "wq": np.ascontiguousarray(np.asarray(L0_attn_Wq, np.float32)),
        "wk": np.ascontiguousarray(np.asarray(L0_attn_Wk, np.float32)),
        "wv": np.ascontiguousarray(np.asarray(L0_attn_Wv, np.float32)),
        "wo": np.ascontiguousarray(np.asarray(L0_attn_Wo, np.float32)),
        "w1t": np.ascontiguousarray(np.asarray(L0_ffn_w1, np.float32).T),
        "w2t": np.ascontiguousarray(np.asarray(L0_ffn_w2, np.float32).T),
        "bq2": np.ascontiguousarray(np.asarray(L0_attn_bq, np.float32).reshape(8, 128).T),
        "bk2": np.ascontiguousarray(np.asarray(L0_attn_bk, np.float32).reshape(8, 128).T),
        "bv2": np.ascontiguousarray(np.asarray(L0_attn_bv, np.float32).reshape(8, 128).T),
        "bor": np.asarray(L0_attn_bo, np.float32).reshape(1, D),
        "b12": np.ascontiguousarray(np.asarray(L0_ffn_b1, np.float32).reshape(32, 128).T),
        "b2r": np.asarray(L0_ffn_b2, np.float32).reshape(1, D),
        "idb": idb, "idf": idf, "maskt": maskt,
    }
    in_maps = []
    for c in range(NCORES):
        m = dict(com)
        m["x"] = np.ascontiguousarray(x[2 * c:2 * c + 2].reshape(T, D))
        in_maps.append(m)
    res = run_bass_kernel_spmd(nc, in_maps, core_ids=list(range(NCORES)))
    out = np.empty((16, S, D), np.float32)
    for c in range(NCORES):
        out[2 * c:2 * c + 2] = np.asarray(res.results[c]["out"]).reshape(2, S, D)
    return out


# version 07 (final)
# speedup vs baseline: 1.0988x; 1.0223x over previous
import numpy as np
from contextlib import ExitStack
import ml_dtypes
import concourse.bass as bass
import concourse.mybir as mybir
from concourse.bass_utils import run_bass_kernel_spmd
from concourse.alu_op_type import AluOpType as OP

F32 = mybir.dt.float32
BF16 = mybir.dt.bfloat16
AF = mybir.ActivationFunctionType
AX = mybir.AxisListType

MAGIC = float(np.float32(1.5 * 2 ** 23))
RD = float(np.float32(1.0) / np.float32(1024.0))
ESHIFT = -12.0
R127 = float(np.float32(1.0) / np.float32(127.0))
S, D, T, HID = 2048, 1024, 4096, 4096
NCORES = 8
STOP = "all"
DEBUG = False


class Ctr:
    __slots__ = ("h", "n")

    def __init__(self, h):
        self.h = h
        self.n = 0

    def inc(self, inst, k=1):
        inst.then_inc(self.h, k)
        self.n += k
        return (self, self.n)

    def tag(self):
        return (self, self.n)


def _w(eng, *tags):
    for t in tags:
        if t is not None and t[1] > 0:
            eng.wait_ge(t[0].h, t[1])


def build():
    nc = bass.Bass("TRN2", target_bir_lowering=False)

    x_h = nc.dram_tensor("x", [T, D], F32, kind="ExternalInput").ap()
    wq_h = nc.dram_tensor("wq", [D, D], F32, kind="ExternalInput").ap()
    wk_h = nc.dram_tensor("wk", [D, D], F32, kind="ExternalInput").ap()
    wv_h = nc.dram_tensor("wv", [D, D], F32, kind="ExternalInput").ap()
    wo_h = nc.dram_tensor("wo", [D, D], F32, kind="ExternalInput").ap()
    w1t_h = nc.dram_tensor("w1t", [D, HID], F32, kind="ExternalInput").ap()
    w2t_h = nc.dram_tensor("w2t", [HID, D], F32, kind="ExternalInput").ap()
    w1s_h = nc.dram_tensor("w1s", [128, HID], F32, kind="ExternalInput").ap()
    w2s_h = nc.dram_tensor("w2s", [512, D], F32, kind="ExternalInput").ap()
    bq2_h = nc.dram_tensor("bq2", [128, 8], F32, kind="ExternalInput").ap()
    bk2_h = nc.dram_tensor("bk2", [128, 8], F32, kind="ExternalInput").ap()
    bv2_h = nc.dram_tensor("bv2", [128, 8], F32, kind="ExternalInput").ap()
    bor_h = nc.dram_tensor("bor", [1, D], F32, kind="ExternalInput").ap()
    b12_h = nc.dram_tensor("b12", [128, 32], F32, kind="ExternalInput").ap()
    b2r_h = nc.dram_tensor("b2r", [1, D], F32, kind="ExternalInput").ap()
    idb_h = nc.dram_tensor("idb", [128, 128], BF16, kind="ExternalInput").ap()
    idf_h = nc.dram_tensor("idf", [128, 128], F32, kind="ExternalInput").ap()
    mk_h = nc.dram_tensor("maskt", [128, 768], BF16, kind="ExternalInput").ap()
    out_h = nc.dram_tensor("out", [T, D], F32, kind="ExternalOutput").ap()
    if DEBUG:
        dbg_h = nc.dram_tensor("dbg", [128, 49408], BF16, kind="ExternalOutput").ap()
        dbgx_h = nc.dram_tensor("dbgx", [T, D], BF16, kind="ExternalOutput").ap()
        dbgf_h = nc.dram_tensor("dbgf", [128, 64], F32, kind="ExternalOutput").ap()
    hscr = nc.dram_tensor("hscr", [HID, T], F32)
    x1scr = nc.dram_tensor("x1scr", [T, D], BF16)
    cci1 = nc.dram_tensor("cci1", [1, 8], F32)
    cco1 = nc.dram_tensor("cco1", [1, 8], F32)
    cci2 = nc.dram_tensor("cci2", [1, 8], F32)
    cco2 = nc.dram_tensor("cco2", [1, 8], F32)

    es = ExitStack()
    with es:
        def sb(name, shape, dt, st=es):
            return st.enter_context(nc.sbuf_tensor(name, shape, dt))

        idb = sb("idb_s", [128, 128], BF16)
        idf = sb("idf_s", [128, 128], F32)
        maskt = sb("maskt_s", [128, 768], BF16)
        bq2 = sb("bq2_s", [128, 8], F32)
        bk2 = sb("bk2_s", [128, 8], F32)
        bv2 = sb("bv2_s", [128, 8], F32)
        bor = sb("bor_s", [1, D], F32)
        b12 = sb("b12_s", [128, 32], F32)
        b1s = sb("b1s", [128, 32], F32)
        b2r = sb("b2r_s", [1, D], F32)
        bop = sb("boprime", [1, D], F32)
        onesf = sb("onesf", [1, 128], F32)
        esh = sb("esh", [128, 1], F32)
        rstd2a = sb("rstd2a", [128, 32], F32)
        negm2a = sb("negm2a", [128, 32], F32)
        redt = sb("redt", [1, 8], F32)
        gmt = sb("gmt", [1, 8], F32)
        sval = sb("sval", [1, 8], F32)
        bvec = sb("bvec", [128, 8], F32)
        lnv = [sb(f"lnv{i}", [128, 8], F32) for i in range(2)]
        svb = sb("svb", [1, 8], F32)
        rstd2b = sb("rstd2b", [128, 32], F32)
        negm2b = sb("negm2b", [128, 32], F32)
        spc = sb("spc", [128, 256], BF16)
        sA1 = sb("sA1", [128, 16], F32)
        qA1 = sb("qA1", [128, 16], F32)
        mA1 = sb("mA1", [128, 16], F32)
        eA1 = sb("eA1", [128, 16], F32)
        m2A1 = sb("m2A1", [128, 16], F32)
        veA1 = sb("veA1", [128, 16], F32)
        rvA1 = sb("rvA1", [128, 16], F32)
        rstdA1 = sb("rstdA1", [128, 16], F32)
        negmA1 = sb("negmA1", [128, 16], F32)
        sA2 = sb("sA2", [128, 32], F32)
        qA2 = sb("qA2", [128, 32], F32)
        mA2 = sb("mA2", [128, 32], F32)
        eA2 = sb("eA2", [128, 32], F32)
        m2A2 = sb("m2A2", [128, 32], F32)
        veA2 = sb("veA2", [128, 32], F32)
        rvA2 = sb("rvA2", [128, 32], F32)
        amxc = sb("amxc", [128, 32], F32)
        amhc = sb("amhc", [128, 256], F32)
        aw1c = sb("aw1c", [128, 8], F32)
        aw2c = sb("aw2c", [128, 32], F32)
        amx1 = sb("amx1", [128, 1], F32)
        amh1 = sb("amh1", [128, 1], F32)
        aw11 = sb("aw11", [128, 1], F32)
        aw21 = sb("aw21", [128, 1], F32)

        def mk(name):
            return Ctr(es.enter_context(nc.semaphore(name)))

        pe, ve, se, ge = mk("pe"), mk("ve"), mk("se"), mk("ge")
        s_x = [mk("s_x0"), mk("s_x1")]
        s_w = [mk("s_w0"), mk("s_w1")]
        s_b = mk("s_b")
        s_x1w = [mk("s_x1w0"), mk("s_x1w1")]
        s_x1r = [mk("s_x1r0"), mk("s_x1r1")]
        s_hw = [mk(f"s_hw{i}") for i in range(3)]
        s_hr = [mk(f"s_hr{i}") for i in range(4)]
        s_cc = mk("s_cc")
        s_out = [mk(f"s_out{i}") for i in range(4)]

        blk = es.enter_context(nc.Block())
        ENGS = {"t": blk.tensor, "v": blk.vector, "s": blk.scalar, "g": blk.gpsimd}
        ECTR = {"t": pe, "v": ve, "s": se, "g": ge}

        def barrier(dma_tags=()):
            tags = []
            for k in ("t", "v", "s", "g"):
                c = ECTR[k]

                def _f(e, c=c):
                    e.drain()
                    c.inc(e.nop())
                ENGS[k](_f)
                tags.append(c.tag())

            def _fy(e, tags=tags, dma_tags=tuple(dma_tags)):
                _w(e, *tags)
                for dt_ in dma_tags:
                    _w(e, dt_)
            for k in ("t", "v", "s", "g"):
                ENGS[k](_fy)
            blk.sync(_fy)

        def finish(dumps=(), hbm_dumps=(), fdumps=()):
            def _d(e):
                if DEBUG:
                    off = 0
                    for srct in dumps:
                        w = srct.shape[1]
                        s_out[0].inc(e.dma_start(out=dbg_h[:, off:off + w], in_=srct[:]), 16)
                        off += w
                    for dst, srcafé in hbm_dumps:
                        s_out[0].inc(e.dma_start(out=dst[:, :], in_=srcafé[:, :]), 16)
                    off = 0
                    for srct in fdumps:
                        w = srct.shape[1]
                        s_out[0].inc(e.dma_start(out=dbgf_h[:, off:off + w], in_=srct[:]), 16)
                        off += w
                _w(e, s_out[0].tag())
            blk.sync(_d)

        scar = sb("scar", [128, 1], F32)

        def sfence(s):
            s.drain()
            return se.inc(s.activation(out=scar[:], in_=esh[:], func=AF.Identity))

        def vfence2(v):
            vfence(v)
            return vfence(v)

        _vfk = [0]

        def vcar(v, inc=True):
            o = 8 * (_vfk[0] % 16)
            _vfk[0] += 1
            ii = v.tensor_copy(out=spc[:, o:o + 8], in_=maskt[:, o:o + 8])
            if inc:
                return ve.inc(ii)
            return None

        def vfence(v):
            v.drain()
            return vcar(v)

        def _pre_y(e):
            for dst, src in ((idb, idb_h), (idf, idf_h), (maskt, mk_h),
                             (bq2, bq2_h), (bk2, bk2_h), (bv2, bv2_h),
                             (bor, bor_h), (b12, b12_h), (b2r, b2r_h)):
                _w(e, s_b.tag())
                s_b.inc(e.dma_start(out=dst[:], in_=src[:, :]), 16)
        blk.sync(_pre_y)

        def _pre_v(v):
            _w(v, s_b.tag())
            v.memset(onesf[:], 1.0)
            v.memset(esh[:], ESHIFT)
            ve.inc(v.memset(spc[:], 0.0))
        blk.vector(_pre_v)

        aes = es.enter_context(ExitStack())
        Qs = sb("Qs", [128, 8 * S], BF16, aes)
        Ks = sb("Ks", [128, 8 * S], BF16, aes)
        Vs = sb("Vs", [128, 16 * 1040], BF16, aes)
        ln1T = sb("ln1T", [128, 8 * S], BF16, aes)
        AOs = ln1T
        wstage = [sb(f"wstg{i}", [128, 512], F32, aes) for i in range(2)]
        wb = sb("wb", [128, 8 * 1024], BF16, aes)
        xin = [sb(f"xin{i}", [128, 1024], F32, aes) for i in range(2)]
        lntl = [sb(f"lntl{i}", [128, 1024], BF16, aes) for i in range(2)]
        ptsb = [sb(f"ptsb{i}", [128, 768], BF16, aes) for i in range(2)]
        qz = [sb(f"qz{i}", [128, 2 * S], BF16, aes) for i in range(2)]
        ptm = [sb(f"ptm{i}", [128, 768], BF16, aes) for i in range(2)]
        rrt = [sb(f"rrt{i}", [128, 2], F32, aes) for i in range(2)]
        x1t = [sb(f"x1t{i}", [128, 1024], BF16, aes) for i in range(2)]

        aps = es.enter_context(ExitStack())
        P0 = aps.enter_context(nc.psum_tensor("P0", [128, 512], BF16))
        P0b = aps.enter_context(nc.psum_tensor("P0b", [128, 512], BF16))
        SC = [aps.enter_context(nc.psum_tensor("SC0", [128, 1024], F32)),
              aps.enter_context(nc.psum_tensor("SC1", [128, 1024], F32))]
        AT = [aps.enter_context(nc.psum_tensor("AT0", [128, 512], F32)),
              aps.enter_context(nc.psum_tensor("AT1", [128, 512], F32))]

        def _vones(v):
            ve.inc(v.memset(Vs[:, 64::65], 1.0))
        blk.vector(_vones)
        barrier(dma_tags=(s_b.tag(),))

        if STOP == "pre":
            barrier()
            finish()
            return nc

        tp_state = {"free": [None, None]}

        def transpose_tile(src, dstT, W, toff, dep_tags, pbanks):
            last = None
            for hh in range(2):
                ps = pbanks[hh]
                t_half = [None]

                def _tr(tn, ps=ps, hh=hh, t_half=t_half, dep_tags=tuple(dep_tags)):
                    _w(tn, *dep_tags, tp_state["free"][hh])
                    for j in range(4):
                        ii = tn.transpose(out=ps[:, 128 * j:128 * (j + 1)],
                                          in_=src[:, 128 * (4 * hh + j):128 * (4 * hh + j + 1)],
                                          identity=idb[:])
                    t_half[0] = pe.inc(ii)
                blk.tensor(_tr)

                def _ev(v, ps=ps, hh=hh, t_half=t_half):
                    _w(v, t_half[0])
                    v.tensor_copy(
                        out=dstT[:].rearrange("p (c n) -> p c n", n=W)[
                            :, 4 * hh:4 * hh + 4, toff:toff + 128],
                        in_=ps[:].rearrange("p (c n) -> p c n", n=128))
                    vfence(v)
                blk.vector(_ev)
                tp_state["free"][hh] = ve.tag()
                last = ve.tag()
            return last

        xin_free = [None, None]
        xin_free2 = [None, None]
        lnt_free = [None, None]
        at_free = [None, None]
        x1_free = [None, None]
        wb_free = None

        for sq in range(2):
            tok0 = sq * S
            for t in range(16):
                sl = t % 2
                row = tok0 + 128 * t
                ld_tag = [None]

                def _ld(e, sl=sl, row=row, ld_tag=ld_tag):
                    _w(e, xin_free[sl], xin_free2[sl])
                    _w(e, s_x[sl].tag())
                    ld_tag[0] = s_x[sl].inc(
                        e.dma_start(out=xin[sl][:], in_=x_h[row:row + 128, :]), 16)
                blk.sync(_ld)

                def _rs(v, sl=sl, t=t, ld_tag=ld_tag):
                    _w(v, ld_tag[0])
                    v.tensor_reduce(out=sA1[:, t:t + 1], in_=xin[sl][:],
                                    axis=AX.X, op=OP.add)
                    vfence2(v)
                blk.vector(_rs)
                xin_free[sl] = ve.tag()

                def _qs(s, sl=sl, t=t, ld_tag=ld_tag):
                    _w(s, ld_tag[0])
                    s.activation(out=lntl[sl][:], in_=xin[sl][:], func=AF.Square,
                                 accum_out=qA1[:, t:t + 1])
                    sfence(s)
                blk.scalar(_qs)
                xin_free2[sl] = se.tag()

            bm_tag = [None]

            def _bm(v):
                _w(v, xin_free[0], xin_free[1], xin_free2[0], xin_free2[1])
                v.tensor_scalar(out=mA1[:], in0=sA1[:], scalar1=RD, scalar2=None, op0=OP.mult)
                v.tensor_scalar(out=eA1[:], in0=qA1[:], scalar1=RD, scalar2=1e-5,
                                op0=OP.mult, op1=OP.add)
                v.drain()
                v.tensor_tensor(out=m2A1[:], in0=mA1[:], in1=mA1[:], op=OP.mult)
                v.drain()
                v.scalar_tensor_tensor(out=veA1[:], in0=m2A1[:], scalar=-1.0, in1=eA1[:],
                                       op0=OP.mult, op1=OP.add)
                v.drain()
                v.reciprocal(out=rvA1[:], in_=veA1[:])
                bm_tag[0] = vfence(v)
            blk.vector(_bm)

            sq_tag = [None]

            def _bs(s):
                _w(s, bm_tag[0], xin_free2[0], xin_free2[1])
                s.activation(out=rstdA1[:], in_=rvA1[:], func=AF.Sqrt)
                sq_tag[0] = sfence(s)
            blk.scalar(_bs)

            ng_tag = [None]

            def _bn2(v):
                _w(v, sq_tag[0])
                v.scalar_tensor_tensor(out=negmA1[:], in0=mA1[:], scalar=-1.0, in1=rstdA1[:],
                                       op0=OP.mult, op1=OP.mult)
                ng_tag[0] = vfence(v)
            blk.vector(_bn2)

            for t in range(16):
                sl = t % 2
                row = tok0 + 128 * t
                ld_tag = [None]

                def _ld2(e, sl=sl, row=row, ld_tag=ld_tag):
                    _w(e, xin_free[sl], xin_free2[sl])
                    _w(e, s_x[sl].tag())
                    ld_tag[0] = s_x[sl].inc(
                        e.dma_start(out=xin[sl][:], in_=x_h[row:row + 128, :]), 16)
                blk.sync(_ld2)

                ap_tag = [None]

                def _ap(s, sl=sl, t=t, ld_tag=ld_tag, ap_tag=ap_tag):
                    _w(s, ld_tag[0], ng_tag[0], lnt_free[sl])
                    s.activation(out=lntl[sl][:], in_=xin[sl][:],
                                 func=AF.Identity,
                                 bias=negmA1[:, t:t + 1],
                                 scale=rstdA1[:, t:t + 1])
                    ap_tag[0] = sfence(s)
                blk.scalar(_ap)
                xin_free[sl] = se.tag()
                lnt_free[sl] = transpose_tile(lntl[sl], ln1T, S, 128 * t,
                                              (ap_tag[0],), (P0, P0b))

            barrier()
            if STOP == "A0":
                finish(dumps=(ln1T,))
                return nc

            def load_w(src_h, dst, wb_free):
                wst_free = [None, None]
                last = [None]
                for ch in range(16):
                    dc, hf = ch // 2, ch % 2
                    sl = ch % 2
                    lt = [None]

                    def _lw(e, sl=sl, dc=dc, hf=hf, lt=lt, src_h=src_h):
                        _w(e, wst_free[sl])
                        _w(e, s_w[sl].tag())
                        lt[0] = s_w[sl].inc(
                            e.dma_start(out=wstage[sl][:],
                                        in_=src_h[128 * dc:128 * (dc + 1),
                                                  512 * hf:512 * (hf + 1)]), 16)
                    blk.sync(_lw)

                    def _cw(v, sl=sl, dc=dc, hf=hf, ch=ch, lt=lt, dst=dst,
                            wb_free=wb_free, last=last):
                        _w(v, lt[0], wb_free if ch == 0 else None)
                        v.tensor_copy(out=dst[:, 1024 * dc + 512 * hf:1024 * dc + 512 * hf + 512],
                                      in_=wstage[sl][:])
                        if ch % 2 == 1 or ch == 15:
                            last[0] = vfence(v)
                    blk.vector(_cw)
                    if ch % 2 == 1 or ch == 15:
                        wst_free[0] = ve.tag()
                        wst_free[1] = ve.tag()
                return last[0]

            def proj_featmajor(dst, bias2d, wtag):
                nonlocal wb_free
                it = 0
                for hp in range(8):
                    for tq in range(4):
                        sl = it % 2
                        mt = [None]

                        def _mm(tn, hp=hp, tq=tq, sl=sl, mt=mt, wtag=wtag):
                            _w(tn, wtag, at_free[sl])
                            for dc in range(8):
                                ii = tn.matmul(
                                    out=AT[sl][:],
                                    lhsT=wb[:, 1024 * dc + 128 * hp:1024 * dc + 128 * hp + 128],
                                    rhs=ln1T[:, S * dc + 512 * tq:S * dc + 512 * tq + 512],
                                    start=(dc == 0), stop=(dc == 7))
                            mt[0] = pe.inc(ii)
                        blk.tensor(_mm)

                        def _evq(s, hp=hp, tq=tq, sl=sl, mt=mt, dst=dst, bias2d=bias2d):
                            _w(s, mt[0])
                            if bias2d is not None:
                                s.activation(
                                    out=dst[:, S * hp + 512 * tq:S * hp + 512 * tq + 512],
                                    in_=AT[sl][:], func=AF.Identity,
                                    bias=bias2d[:, hp:hp + 1], scale=1.0)
                            else:
                                s.activation(
                                    out=dst[:, S * hp + 512 * tq:S * hp + 512 * tq + 512],
                                    in_=AT[sl][:], func=AF.Identity)
                            sfence(s)
                        blk.scalar(_evq)
                        at_free[sl] = se.tag()
                        it += 1
                wb_free = pe.tag()

            wtag = load_w(wq_h, wb, wb_free)
            proj_featmajor(Qs, bq2, wtag)
            barrier()
            if STOP == "Bq":
                barrier()
                finish(dumps=(Qs,))
                return nc
            wtag = load_w(wk_h, wb, wb_free)
            proj_featmajor(Ks, bk2, wtag)
            barrier()
            if STOP == "Bk":
                barrier()
                finish(dumps=(Qs, Ks))
                return nc
            wtag = load_w(wv_h, wb, wb_free)
            it = 0
            for tt in range(16):
                for hf in range(2):
                    sl = it % 2
                    mt = [None]

                    def _mmv(tn, tt=tt, hf=hf, sl=sl, mt=mt, wtag=wtag):
                        _w(tn, wtag, at_free[sl])
                        for dc in range(8):
                            ii = tn.matmul(
                                out=AT[sl][:],
                                lhsT=ln1T[:, S * dc + 128 * tt:S * dc + 128 * tt + 128],
                                rhs=wb[:, 1024 * dc + 512 * hf:1024 * dc + 512 * hf + 512],
                                start=(dc == 0), stop=(dc == 7))
                        mt[0] = pe.inc(ii)
                    blk.tensor(_mmv)

                    def _evv(s, tt=tt, hf=hf, sl=sl, mt=mt):
                        _w(s, mt[0])
                        s.activation(
                            out=Vs[:].rearrange("p (a b) -> p a b", b=65)[
                                :, 16 * tt + 8 * hf:16 * tt + 8 * hf + 8, 0:64],
                            in_=AT[sl][:].rearrange("p (a b) -> p a b", b=64),
                            func=AF.Identity)
                        sfence(s)
                    blk.scalar(_evv)
                    at_free[sl] = se.tag()
                    it += 1
            wb_free = pe.tag()
            barrier()

            wst_free2 = [None, None]
            bo_mm = [None]
            for dc in range(8):
                for hf in range(2):
                    sl = (2 * dc + hf) % 2
                    lt = [None]

                    def _lw(e, sl=sl, dc=dc, hf=hf, lt=lt):
                        for tg in (wst_free2[sl] or ()):
                            _w(e, tg)
                        _w(e, s_w[sl].tag())
                        lt[0] = s_w[sl].inc(
                            e.dma_start(out=wstage[sl][:],
                                        in_=wo_h[128 * dc:128 * (dc + 1),
                                                 512 * hf:512 * (hf + 1)]), 16)
                    blk.sync(_lw)

                    ct_ = [None]

                    def _cw(v, sl=sl, dc=dc, hf=hf, lt=lt, ct_=ct_):
                        _w(v, lt[0], wb_free if dc == 0 and hf == 0 else None)
                        v.tensor_copy(
                            out=wb[:, 1024 * dc + 512 * hf:1024 * dc + 512 * hf + 512],
                            in_=wstage[sl][:])
                        ct_[0] = vfence(v)
                    blk.vector(_cw)

                    if sq == 0:
                        def _bo(tn, sl=sl, dc=dc, hf=hf, lt=lt, bo_mm=bo_mm):
                            _w(tn, lt[0])
                            bo_mm[0] = pe.inc(tn.matmul(
                                out=SC[0][0:1, 512 * hf:512 * hf + 512],
                                lhsT=bv2[:, dc:dc + 1],
                                rhs=wstage[sl][:, 0:512], start=(dc == 0), stop=(dc == 7)))
                        blk.tensor(_bo)
                        wst_free2[sl] = (ct_[0], pe.tag())
                    else:
                        wst_free2[sl] = (ct_[0],)
            wotag = ve.tag()

            if sq == 0:
                def _boev(v, bo_mm=bo_mm):
                    _w(v, bo_mm[0])
                    v.tensor_tensor(out=bop[:], in0=SC[0][0:1, :], in1=bor[:],
                                    op=OP.add)
                    vfence(v)
                blk.vector(_boev)

            barrier()
            if STOP == "B0":
                finish(dumps=(Qs, Ks, Vs))
                return nc

            sc_free = [None, None]
            pt_free = [None, None]
            ptm_free = [None, None]
            atp_free = [None, None]
            prev_ev = [None]
            qz_free = [None, None]
            qz_tag = [None, None]
            it = 0
            for hp in range(8):
                qsl = hp % 2
                qztag = [None]

                def _qz(v, hp=hp, qsl=qsl, qztag=qztag):
                    _w(v, qz_free[qsl])
                    v.tensor_copy(out=qz[qsl][0:64, 0:S], in_=Qs[0:64, S * hp:S * (hp + 1)])
                    v.memset(qz[qsl][64:128, 0:S], 0.0)
                    v.tensor_copy(out=qz[qsl][64:128, S:2 * S],
                                  in_=Qs[64:128, S * hp:S * (hp + 1)])
                    v.memset(qz[qsl][0:64, S:2 * S], 0.0)
                    qztag[0] = vfence(v)
                blk.vector(_qz)
                qz_tag[qsl] = qztag[0]

                for qb in range(16):
                    sl = it % 2
                    if qb == 0:
                        tcs, moff = [0, 1], 128
                    elif qb == 15:
                        tcs, moff = [14, 15], 0
                    else:
                        tcs, moff = [qb - 1, qb, qb + 1], None
                    nch = len(tcs)
                    smt = [None]

                    def _sco(tn, hp=hp, qb=qb, sl=sl, qsl=qsl, tcs=tcs, smt=smt,
                             qztag=qztag):
                        _w(tn, sc_free[sl], qztag[0] if qb == 0 else None)
                        for hh in range(2):
                            for ci in range(len(tcs)):
                                tc = tcs[ci]
                                ii = tn.matmul(
                                    out=SC[sl][:, 384 * hh + 128 * ci:384 * hh + 128 * ci + 128],
                                    lhsT=Ks[:, S * hp + 128 * tc:S * hp + 128 * tc + 128],
                                    rhs=qz[qsl][:, S * hh + 128 * qb:S * hh + 128 * qb + 128],
                                    start=True, stop=True)
                        smt[0] = pe.inc(ii)
                    blk.tensor(_sco)

                    ext = [None]

                    def _exp(s, sl=sl, nch=nch, smt=smt, ext=ext):
                        _w(s, smt[0], pt_free[sl])
                        if nch == 3:
                            ii = s.activation(out=ptsb[sl][:], in_=SC[sl][:, 0:768],
                                              func=AF.Exp, bias=esh[:], scale=0.125)

                        else:
                            s.activation(out=ptsb[sl][:, 0:256], in_=SC[sl][:, 0:256],
                                         func=AF.Exp, bias=esh[:], scale=0.125)
                            ii = s.activation(out=ptsb[sl][:, 384:640], in_=SC[sl][:, 384:640],
                                              func=AF.Exp, bias=esh[:], scale=0.125)
                        ext[0] = sfence(s)
                    blk.scalar(_exp)
                    sc_free[sl] = se.tag()

                    mkt = [None]

                    def _msk(v, sl=sl, nch=nch, moff=moff, ext=ext, mkt=mkt):
                        _w(v, ext[0], ptm_free[sl])
                        if nch == 3:
                            v.tensor_tensor(out=ptm[sl][:], in0=ptsb[sl][:],
                                            in1=maskt[:], op=OP.mult)
                        else:
                            po = ptm[sl][:].rearrange("p (a b) -> p a b", b=384)[:, :, 0:256]
                            pv = ptsb[sl][:].rearrange("p (a b) -> p a b", b=384)[:, :, 0:256]
                            mv = maskt[:].rearrange("p (a b) -> p a b", b=384)[:, :, moff:moff + 256]
                            v.tensor_tensor(out=po, in0=pv, in1=mv, op=OP.mult)
                        v.drain()
                        mkt[0] = vcar(v)
                    blk.vector(_msk)

                    pvt = [None]

                    pt_free[sl] = ve.tag()

                    def _pv(tn, hp=hp, sl=sl, tcs=tcs, mkt=mkt, pvt=pvt):
                        _w(tn, mkt[0], atp_free[sl])
                        for hh in range(2):
                            h = 2 * hp + hh
                            for ci in range(len(tcs)):
                                tc = tcs[ci]
                                ii = tn.matmul(
                                    out=AT[sl][:, 65 * hh:65 * hh + 65],
                                    lhsT=ptm[sl][:, 384 * hh + 128 * ci:384 * hh + 128 * ci + 128],
                                    rhs=Vs[:, 1040 * tc + 65 * h:1040 * tc + 65 * h + 65],
                                    start=(ci == 0), stop=(ci == len(tcs) - 1))
                        pvt[0] = pe.inc(ii)
                    blk.tensor(_pv)
                    ptm_free[sl] = pe.tag()

                    def _aev(v, hp=hp, qb=qb, sl=sl, pvt=pvt):
                        _w(v, pvt[0])
                        v.reciprocal(out=rrt[sl][:], in_=AT[sl][:, 64:130:65])
                        pv = prev_ev[0]
                        if pv is not None:
                            php, pqb, psl = pv
                            v.tensor_scalar(
                                out=AOs[:, 1024 * pqb + 128 * php:1024 * pqb + 128 * php + 64],
                                in0=AT[psl][:, 0:64], scalar1=rrt[psl][:, 0:1],
                                scalar2=None, op0=OP.mult)
                            v.tensor_scalar(
                                out=AOs[:, 1024 * pqb + 128 * php + 64:1024 * pqb + 128 * php + 128],
                                in0=AT[psl][:, 65:129], scalar1=rrt[psl][:, 1:2],
                                scalar2=None, op0=OP.mult)
                            vfence(v)
                            atp_free[psl] = ve.tag()
                        else:
                            vcar(v)
                        prev_ev[0] = (hp, qb, sl)
                    blk.vector(_aev)
                    it += 1
                qz_free[qsl] = pe.tag()

            def _aev_fl(v):
                pv = prev_ev[0]
                php, pqb, psl = pv
                v.drain()
                v.tensor_scalar(
                    out=AOs[:, 1024 * pqb + 128 * php:1024 * pqb + 128 * php + 64],
                    in0=AT[psl][:, 0:64], scalar1=rrt[psl][:, 0:1],
                    scalar2=None, op0=OP.mult)
                v.tensor_scalar(
                    out=AOs[:, 1024 * pqb + 128 * php + 64:1024 * pqb + 128 * php + 128],
                    in0=AT[psl][:, 65:129], scalar1=rrt[psl][:, 1:2],
                    scalar2=None, op0=OP.mult)
                vfence(v)
                prev_ev[0] = None
            blk.vector(_aev_fl)

            barrier()
            if STOP == "C0":
                finish(dumps=(AOs,))
                return nc

            for t in range(16):
                sl = t % 2
                row = tok0 + 128 * t
                ld_tag = [None]

                def _ld(e, sl=sl, row=row, ld_tag=ld_tag):
                    _w(e, xin_free[sl])
                    _w(e, s_x[sl].tag())
                    ld_tag[0] = s_x[sl].inc(
                        e.dma_start(out=xin[sl][:], in_=x_h[row:row + 128, :]), 16)
                blk.sync(_ld)

                aot_tag = transpose_tile(AOs[:, 1024 * t:1024 * (t + 1)], lntl[sl], 128, 0,
                                         (lnt_free[sl],), (P0, P0b))
                om_last = None
                for hf in range(2):
                    psl = hf
                    mt = [None]

                    def _om(tn, hf=hf, sl=sl, psl=psl, mt=mt, aot_tag=aot_tag, t=t):
                        _w(tn, aot_tag, at_free[psl], wotag if t == 0 else None)
                        for dc in range(8):
                            tn.matmul(out=AT[psl][:],
                                      lhsT=lntl[sl][:, 128 * dc:128 * (dc + 1)],
                                      rhs=wb[:, 1024 * dc + 512 * hf:1024 * dc + 512 * hf + 512],
                                      start=(dc == 0), stop=False)
                        mt[0] = pe.inc(tn.matmul(
                            out=AT[psl][:], lhsT=onesf[0:1, :],
                            rhs=bop[0:1, 512 * hf:512 * hf + 512], start=False, stop=True))
                    blk.tensor(_om)

                    def _orv(v, hf=hf, sl=sl, psl=psl, mt=mt, ld_tag=ld_tag):
                        _w(v, mt[0], ld_tag[0], x1_free[sl] if hf == 0 else None)
                        v.tensor_tensor(
                            out=x1t[sl][:, 512 * hf:512 * hf + 512], in0=AT[psl][:],
                            in1=xin[sl][:, 512 * hf:512 * hf + 512], op=OP.add)
                        vfence(v)
                    blk.vector(_orv)
                    at_free[psl] = ve.tag()
                lnt_free[sl] = pe.tag()
                xin_free[sl] = ve.tag()

                ti = sq * 16 + t

                def _st2(v, sl=sl, ti=ti):
                    v.tensor_reduce(out=sA2[:, ti:ti + 1], in_=x1t[sl][:],
                                    axis=AX.X, op=OP.add)
                    vfence2(v)
                blk.vector(_st2)
                red_done = ve.tag()

                def _sq2(s, sl=sl, ti=ti, red_done=red_done):
                    _w(s, red_done)
                    s.activation(out=xin[sl][:], in_=x1t[sl][:], func=AF.Square,
                                 accum_out=qA2[:, ti:ti + 1])
                    sfence(s)
                blk.scalar(_sq2)
                stats_done = se.tag()
                xin_free[sl] = se.tag()

                def _sp(g, sl=sl, row=row, stats_done=stats_done):
                    _w(g, stats_done)
                    _w(g, s_x1w[sl].tag())
                    s_x1w[sl].inc(g.dma_start(out=x1scr[row:row + 128, :], in_=x1t[sl][:]), 16)
                blk.gpsimd(_sp)
                x1_free[sl] = s_x1w[sl].tag()

            barrier(dma_tags=(s_x1w[0].tag(), s_x1w[1].tag()))
            if STOP == "D0" and sq == 0:
                finish(hbm_dumps=((dbgx_h, x1scr),) if DEBUG else ())
                return nc

        if STOP == "attn":
            def _fatt(e):
                if DEBUG:
                    s_out[0].inc(e.dma_start(out=dbgx_h[:, :], in_=x1scr[:, :]), 16)
                _w(e, s_out[0].tag())
            blk.sync(_fatt)
            aps.close()
            aes.close()
            return nc

        aps.close()
        aes.close()

        fesb = es.enter_context(ExitStack())
        x1r = [sb(f"x1r{i}", [128, 1024], BF16, fesb) for i in range(2)]
        ln2f = [sb(f"ln2f{i}", [128, 1024], F32, fesb) for i in range(2)]
        w2st = [sb(f"w2st{i}", [128, 1024], F32, fesb) for i in range(2)]
        fes = es.enter_context(ExitStack())
        w1st = [sb(f"w1st{i}", [128, 4096], F32, fes) for i in range(2)]
        w1qT = sb("w1qT", [128, 8 * 4096], BF16, fes)
        xqT = sb("xqT", [128, 8 * 4096], BF16, fes)
        xqtl = [sb(f"xqtl{i}", [128, 1024], BF16, fes) for i in range(2)]
        htl = [sb(f"htl{i}", [128, 512], F32, fes) for i in range(2)]

        fps = es.enter_context(ExitStack())
        PF0 = fps.enter_context(nc.psum_tensor("PF0", [128, 512], BF16))
        PF0b = fps.enter_context(nc.psum_tensor("PF0b", [128, 512], BF16))
        MM = [fps.enter_context(nc.psum_tensor(f"MM{i}", [128, 512], F32)) for i in range(3)]
        UP = fps.enter_context(nc.psum_tensor("UP", [128, 128], F32))
        tp_state["free"] = [None, None]

        bm2_tag = [None]

        def _bm2(v):
            v.tensor_scalar(out=mA2[:], in0=sA2[:], scalar1=RD, scalar2=None, op0=OP.mult)
            v.tensor_scalar(out=eA2[:], in0=qA2[:], scalar1=RD, scalar2=1e-5,
                            op0=OP.mult, op1=OP.add)
            v.drain()
            v.tensor_tensor(out=m2A2[:], in0=mA2[:], in1=mA2[:], op=OP.mult)
            v.drain()
            v.scalar_tensor_tensor(out=veA2[:], in0=m2A2[:], scalar=-1.0, in1=eA2[:],
                                   op0=OP.mult, op1=OP.add)
            v.drain()
            v.reciprocal(out=rvA2[:], in_=veA2[:])
            bm2_tag[0] = vfence(v)
        blk.vector(_bm2)

        fin_tag = [None]

        def _fin_s(s):
            _w(s, bm2_tag[0])
            s.activation(out=rstd2b[:], in_=rvA2[:], func=AF.Sqrt)
            fin_tag[0] = sfence(s)
        blk.scalar(_fin_s)

        ln2fin = [None]

        def _fin_v(v):
            _w(v, fin_tag[0])
            v.scalar_tensor_tensor(out=negm2b[:], in0=mA2[:], scalar=-1.0, in1=rstd2b[:],
                                   op0=OP.mult, op1=OP.mult)
            ln2fin[0] = vfence(v)
        blk.vector(_fin_v)

        x1r_free = [None, None]
        ln2f_free = [None, None]
        for t in range(32):
            sl = t % 2
            lt = [None]

            def _l(e, sl=sl, t=t, lt=lt):
                _w(e, x1r_free[sl])
                _w(e, s_x1r[sl].tag())
                lt[0] = s_x1r[sl].inc(
                    e.dma_start(out=x1r[sl][:], in_=x1scr[128 * t:128 * (t + 1), :]), 16)
            blk.sync(_l)

            apt = [None]

            def _a(s, sl=sl, t=t, lt=lt, apt=apt):
                _w(s, lt[0], ln2fin[0], ln2f_free[sl])
                s.activation(out=ln2f[sl][:], in_=x1r[sl][:], func=AF.Identity,
                             bias=negm2b[:, t:t + 1], scale=rstd2b[:, t:t + 1])
                apt[0] = sfence(s)
            blk.scalar(_a)
            x1r_free[sl] = se.tag()

            def _m(v, sl=sl, t=t, apt=apt):
                _w(v, apt[0])
                v.tensor_reduce(out=amxc[:, t:t + 1], in_=ln2f[sl][:], axis=AX.X,
                                op=OP.max, apply_absolute_value=True)
                vfence(v)
            blk.vector(_m)
            ln2f_free[sl] = ve.tag()

        def _mfin(v):
            v.drain()
            v.tensor_reduce(out=amx1[:], in_=amxc[:], axis=AX.X, op=OP.max)
            vfence2(v)
        blk.vector(_mfin)
        amx_done = ve.tag()

        w1_free = [None, None]
        lt1 = [None]

        def _lw1(e, lt=lt1):
            _w(e, s_w[0].tag())
            lt[0] = s_w[0].inc(
                e.dma_start(out=w1st[0][:], in_=w1s_h[:, :]), 16)
        blk.sync(_lw1)

        def _mw1(v, lt=lt1):
            _w(v, lt[0])
            v.tensor_reduce(out=aw11[:], in_=w1st[0][:], axis=AX.X,
                            op=OP.max, apply_absolute_value=True)
            vfence2(v)
        blk.vector(_mw1)
        aw1_done = ve.tag()
        w1_free[0] = ve.tag()

        w2_free = [None, None]
        for hc in range(4):
            sl = hc % 2
            lt = [None]

            def _l(e, sl=sl, hc=hc, lt=lt):
                _w(e, w2_free[sl])
                _w(e, s_w[sl].tag())
                lt[0] = s_w[sl].inc(
                    e.dma_start(out=w2st[sl][:], in_=w2s_h[128 * hc:128 * (hc + 1), :]), 16)
            blk.sync(_l)

            def _m(v, sl=sl, hc=hc, lt=lt):
                _w(v, lt[0])
                v.tensor_reduce(out=aw2c[:, hc:hc + 1], in_=w2st[sl][:], axis=AX.X,
                                op=OP.max, apply_absolute_value=True)
                vfence(v)
            blk.vector(_m)
            w2_free[sl] = ve.tag()

        def _w2fin(v):
            v.drain()
            v.tensor_reduce(out=aw21[:], in_=aw2c[:, 0:4], axis=AX.X, op=OP.max)
            vfence2(v)
        blk.vector(_w2fin)
        aw2_done = ve.tag()

        if STOP == "absmax":
            barrier()
            finish(fdumps=(amx, aw1, aw2))
            return nc

        def part_max(src, dstcol, dep):
            tt_ = [None]

            def _t(tn, tt_=tt_, dep=dep):
                _w(tn, *dep)
                tt_[0] = pe.inc(tn.transpose(out=UP[0:1, 0:128], in_=src, identity=idf[:]))
            blk.tensor(_t)

            def _r(v, tt_=tt_, dstcol=dstcol):
                _w(v, tt_[0])
                v.tensor_reduce(out=sval[0:1, dstcol:dstcol + 1], in_=UP[0:1, 0:128],
                                axis=AX.X, op=OP.max)
                vfence2(v)
            blk.vector(_r)
            return ve.tag()

        t1 = part_max(amx1[:], 0, (amx_done,))
        t2 = part_max(aw11[:], 1, (aw1_done, t1))
        t3 = part_max(aw21[:], 2, (aw2_done, t2))

        cc1_tag = [None]

        def _prep_cc2(v):
            _w(v, t3)
            for j in range(8):
                src = 0 if j < 4 else (1 if j < 6 else 2)
                v.tensor_copy(out=redt[0:1, j:j + 1], in_=sval[0:1, src:src + 1])
            cc1_tag[0] = vfence(v)
        blk.vector(_prep_cc2)

        gm_tag = [None]

        def _cc1(g):
            _w(g, cc1_tag[0])
            _w(g, s_cc.tag())
            s_cc.inc(g.dma_start(out=cci1[:, :], in_=redt[:]), 16)
            _w(g, s_cc.tag())
            ge.inc(g.collective_compute(
                "AllReduce", mybir.AluOpType.max,
                replica_groups=[list(range(NCORES))],
                ins=[cci1.ap().opt()], outs=[cco1.ap().opt()]))
            _w(g, ge.tag())
            _w(g, s_cc.tag())
            gm_tag[0] = s_cc.inc(g.dma_start(out=gmt[:], in_=cco1[:, :]), 16)
        blk.gpsimd(_cc1)

        sc_tag = [None]

        def _scl(v):
            _w(v, gm_tag[0])
            def SP():
                v.drain()
            v.tensor_reduce(out=svb[0:1, 0:1], in_=gmt[0:1, 0:4], axis=AX.X, op=OP.max)
            _w(v, vfence2(v))
            v.tensor_scalar(out=svb[0:1, 1:2], in0=gmt[0:1, 4:5], scalar1=1e-6,
                            scalar2=R127, op0=OP.max, op1=OP.mult)
            v.tensor_scalar(out=svb[0:1, 2:3], in0=gmt[0:1, 6:7], scalar1=1e-6,
                            scalar2=R127, op0=OP.max, op1=OP.mult)
            v.tensor_scalar(out=svb[0:1, 3:4], in0=svb[0:1, 0:1], scalar1=1e-6,
                            scalar2=R127, op0=OP.max, op1=OP.mult)
            _w(v, vfence(v))
            v.tensor_copy(out=sval[0:1, 3:4], in_=svb[0:1, 3:4])
            v.reciprocal(out=sval[0:1, 4:5], in_=svb[0:1, 3:4])
            v.tensor_tensor(out=sval[0:1, 5:6], in0=svb[0:1, 1:2], in1=svb[0:1, 3:4],
                            op=OP.mult)
            v.reciprocal(out=sval[0:1, 6:7], in_=svb[0:1, 1:2])
            v.reciprocal(out=sval[0:1, 7:8], in_=svb[0:1, 2:3])
            v.drain()
            sc_tag[0] = ve.inc(v.tensor_copy(out=svb[0:1, 7:8], in_=sval[0:1, 7:8]))
        blk.vector(_scl)

        bc_tag = [None]

        def _bc(tn):
            _w(tn, sc_tag[0])
            bc_tag[0] = pe.inc(tn.matmul(out=UP[:, 0:8], lhsT=onesf[0:1, :],
                                         rhs=sval[0:1, 0:8], start=True, stop=True))
        blk.tensor(_bc)

        bvec_tag = [None]

        def _bv(v):
            _w(v, bc_tag[0])
            v.tensor_copy(out=bvec[:], in_=UP[:, 0:8])
            v.drain()
            v.drain()
            v.tensor_scalar(out=b1s[:], in0=b12[:], scalar1=bvec[:, 5:6], scalar2=None,
                            op0=OP.mult)
            bvec_tag[0] = vfence(v)
        blk.vector(_bv)

        if STOP == "cc1":
            barrier()
            finish(fdumps=(bvec,))
            return nc

        w1_free = [None, None]
        ln2ff = [ln2f_free[0], ln2f_free[1]]
        for dc in range(8):
            sl = dc % 2
            lt = [None]

            def _l(e, sl=sl, dc=dc, lt=lt):
                _w(e, w1_free[sl])
                _w(e, s_w[sl].tag())
                lt[0] = s_w[sl].inc(
                    e.dma_start(out=w1st[sl][:], in_=w1t_h[128 * dc:128 * (dc + 1), :]), 16)
            blk.sync(_l)

            for qq in range(4):
                sl2 = (dc * 4 + qq) % 2
                aq = [None]

                def _q1(s, sl=sl, sl2=sl2, qq=qq, lt=lt, aq=aq):
                    _w(s, lt[0], bvec_tag[0], ln2ff[sl2])
                    s.activation(out=ln2f[sl2][:],
                                 in_=w1st[sl][:, 1024 * qq:1024 * (qq + 1)],
                                 func=AF.Identity, bias=0.0,
                                 scale=bvec[:, 6:7])
                    aq[0] = sfence(s)
                blk.scalar(_q1)

                def _q2(v, sl2=sl2, dc=dc, qq=qq, aq=aq):
                    _w(v, aq[0])
                    v.tensor_scalar(
                        out=w1qT[:, 4096 * dc + 1024 * qq:4096 * dc + 1024 * (qq + 1)],
                        in0=ln2f[sl2][:], scalar1=MAGIC, scalar2=-MAGIC,
                        op0=OP.add, op1=OP.add)
                    if dc == 7 and qq == 3:
                        vfence(v)
                    else:
                        vcar(v)
                blk.vector(_q2)
                ln2ff[sl2] = ve.tag()
            w1_free[sl] = ve.tag()
        w1q_done = ve.tag()
        ln2f_free = [w1q_done, w1q_done]

        if STOP == "w1q":
            barrier()
            finish(dumps=(w1qT[:, 0:16384],))
            return nc

        xqv_tag = [None]

        def _xqv(v):
            _w(v, bvec_tag[0])
            v.tensor_scalar(out=rstd2a[:], in0=rstd2b[:], scalar1=bvec[:, 4:5],
                            scalar2=None, op0=OP.mult)
            v.tensor_scalar(out=negm2a[:], in0=negm2b[:],
                            scalar1=bvec[:, 4:5], scalar2=None, op0=OP.mult)
            xqv_tag[0] = vfence(v)
        blk.vector(_xqv)

        xq_free = [None, None]
        for t in range(32):
            sl = t % 2
            lt = [None]

            def _l(e, sl=sl, t=t, lt=lt):
                _w(e, x1r_free[sl])
                _w(e, s_x1r[sl].tag())
                lt[0] = s_x1r[sl].inc(
                    e.dma_start(out=x1r[sl][:], in_=x1scr[128 * t:128 * (t + 1), :]), 16)
            blk.sync(_l)

            apt = [None]

            def _a(s, sl=sl, t=t, lt=lt, apt=apt):
                _w(s, lt[0], xqv_tag[0], ln2f_free[sl])
                s.activation(out=ln2f[sl][:], in_=x1r[sl][:], func=AF.Identity,
                             bias=negm2a[:, t:t + 1], scale=rstd2a[:, t:t + 1])
                apt[0] = sfence(s)
            blk.scalar(_a)
            x1r_free[sl] = se.tag()

            rq = [None]

            def _r(v, sl=sl, apt=apt, rq=rq):
                _w(v, apt[0], xq_free[sl])
                v.tensor_scalar(out=xqtl[sl][:], in0=ln2f[sl][:],
                                scalar1=MAGIC, scalar2=-MAGIC,
                                op0=OP.add, op1=OP.add)
                rq[0] = vfence(v)
            blk.vector(_r)
            ln2f_free[sl] = ve.tag()
            xq_free[sl] = transpose_tile(xqtl[sl], xqT, T, 128 * t, (rq[0],), (PF0, PF0b))
        xq_done = ve.tag()

        if STOP == "xq":
            barrier()
            finish(dumps=(xqT[:, 0:16384],))
            return nc

        mm_free = [None, None, None]
        htl_free = [None, None]
        it = 0
        for tt in range(8):
            for hc in range(32):
                psl = it % 3
                hsl = it % 2
                mt = [None]

                def _mm1(tn, tt=tt, hc=hc, psl=psl, mt=mt):
                    _w(tn, w1q_done if it == 0 else None, xq_done if it == 0 else None,
                       mm_free[psl])
                    for dc in range(8):
                        ii = tn.matmul(
                            out=MM[psl][:],
                            lhsT=w1qT[:, 4096 * dc + 128 * hc:4096 * dc + 128 * hc + 128],
                            rhs=xqT[:, T * dc + 512 * tt:T * dc + 512 * tt + 512],
                            start=(dc == 0), stop=(dc == 7))
                    mt[0] = pe.inc(ii)
                blk.tensor(_mm1)

                gt_ = [None]

                def _gel(s, hc=hc, psl=psl, hsl=hsl, mt=mt, gt_=gt_):
                    _w(s, mt[0])
                    if htl_free[hsl] is not None:
                        _w(s, *htl_free[hsl])
                    s.activation(out=htl[hsl][:], in_=MM[psl][:],
                                 func=AF.Gelu, bias=b1s[:, hc:hc + 1],
                                 scale=bvec[:, 5:6])
                    gt_[0] = sfence(s)
                blk.scalar(_gel)
                mm_free[psl] = se.tag()

                am_t = [None]

                def _am(v, hsl=hsl, it=it, gt_=gt_, am_t=am_t):
                    _w(v, gt_[0])
                    v.tensor_reduce(out=amhc[:, it:it + 1], in_=htl[hsl][:],
                                    axis=AX.X, op=OP.max,
                                    apply_absolute_value=True)
                    am_t[0] = vfence2(v)
                blk.vector(_am)

                def _spl(g, tt=tt, hc=hc, hsl=hsl, gt_=gt_):
                    _w(g, gt_[0], s_hw[hsl].tag())
                    s_hw[hsl].inc(g.dma_start(
                        out=hscr[128 * hc:128 * (hc + 1), 512 * tt:512 * (tt + 1)],
                        in_=htl[hsl][:]), 16)
                blk.gpsimd(_spl)
                htl_free[hsl] = (am_t[0], s_hw[hsl].tag())
                it += 1

        barrier(dma_tags=tuple(q.tag() for q in s_hw))
        if STOP == "mm1":
            finish(fdumps=(amh,))
            return nc
        fps.close()
        fes.close()

        f2ps = es.enter_context(ExitStack())
        GG = [f2ps.enter_context(nc.psum_tensor(f"GG{i}", [128, 1024], F32)) for i in range(3)]
        UP2 = f2ps.enter_context(nc.psum_tensor("UP2", [128, 128], F32))

        f2es = es.enter_context(ExitStack())
        w2qT = sb("w2qT", [128, 32 * 1024], BF16, f2es)
        hrel = [sb(f"hrel{i}", [128, 512], F32, f2es) for i in range(4)]
        hqf = [sb(f"hqf{i}", [128, 512], F32, f2es) for i in range(4)]
        hqtl = [sb(f"hqtl{i}", [128, 512], BF16, f2es) for i in range(4)]
        x1r2 = x1r
        outt = [sb(f"outt{i}", [128, 512], F32, f2es) for i in range(4)]

        th = part_max_f2 = None

        def part_max2(src, dstcol, dep):
            tt_ = [None]

            def _t(tn, tt_=tt_, dep=dep):
                _w(tn, *dep)
                tt_[0] = pe.inc(tn.transpose(out=UP2[0:1, 0:128], in_=src, identity=idf[:]))
            blk.tensor(_t)

            def _r(v, tt_=tt_, dstcol=dstcol):
                _w(v, tt_[0])
                v.tensor_reduce(out=sval[0:1, dstcol:dstcol + 1], in_=UP2[0:1, 0:128],
                                axis=AX.X, op=OP.max)
                vfence2(v)
            blk.vector(_r)
            return ve.tag()

        def _hfin(v):
            v.drain()
            v.tensor_reduce(out=amh1[:], in_=amhc[:], axis=AX.X, op=OP.max)
            vfence2(v)
        blk.vector(_hfin)
        th = part_max2(amh1[:], 0, (ve.tag(),))
        cc2_tag = [None]

        def _prep_cc2b(v):
            _w(v, th)
            for j in range(8):
                v.tensor_copy(out=redt[0:1, j:j + 1], in_=sval[0:1, 0:1])
            cc2_tag[0] = vfence(v)
        blk.vector(_prep_cc2b)

        gm2_tag = [None]

        def _cc2(g):
            _w(g, cc2_tag[0])
            _w(g, s_cc.tag())
            s_cc.inc(g.dma_start(out=cci2[:, :], in_=redt[:]), 16)
            _w(g, s_cc.tag())
            ge.inc(g.collective_compute(
                "AllReduce", mybir.AluOpType.max,
                replica_groups=[list(range(NCORES))],
                ins=[cci2.ap().opt()], outs=[cco2.ap().opt()]))
            _w(g, ge.tag())
            _w(g, s_cc.tag())
            gm2_tag[0] = s_cc.inc(g.dma_start(out=gmt[:], in_=cco2[:, :]), 16)
        blk.gpsimd(_cc2)

        sc2_tag = [None]

        def _scl2(v):
            _w(v, gm2_tag[0])
            def SP():
                v.drain()
            v.tensor_reduce(out=svb[0:1, 0:1], in_=gmt[:], axis=AX.X, op=OP.max)
            _w(v, vfence2(v))
            v.tensor_scalar(out=svb[0:1, 3:4], in0=svb[0:1, 0:1], scalar1=1e-6,
                            scalar2=R127, op0=OP.max, op1=OP.mult)
            vcar(v, inc=False)
            SP()
            v.reciprocal(out=sval[0:1, 4:5], in_=svb[0:1, 3:4])
            v.tensor_tensor(out=sval[0:1, 5:6], in0=svb[0:1, 2:3],
                            in1=svb[0:1, 3:4], op=OP.mult)
            v.drain()
            sc2_tag[0] = ve.inc(v.tensor_copy(out=svb[0:1, 7:8], in_=sval[0:1, 4:5]))
        blk.vector(_scl2)

        bc2_tag = [None]

        def _bc2(tn):
            _w(tn, sc2_tag[0])
            bc2_tag[0] = pe.inc(tn.matmul(out=UP2[:, 0:8], lhsT=onesf[0:1, :],
                                          rhs=sval[0:1, 0:8], start=True, stop=True))
        blk.tensor(_bc2)

        bv2_tag = [None]

        def _bv2(v):
            _w(v, bc2_tag[0])
            v.tensor_copy(out=bvec[:], in_=UP2[:, 0:8])
            bv2_tag[0] = vfence(v)
        blk.vector(_bv2)

        if STOP == "cc2":
            barrier()
            finish(fdumps=(bvec,))
            return nc

        w2_free = [None, None]
        hq_stage_free = [None, None]
        for hc in range(32):
            sl = hc % 2
            lt = [None]

            def _l(e, sl=sl, hc=hc, lt=lt):
                _w(e, w2_free[sl])
                _w(e, s_w[sl].tag())
                lt[0] = s_w[sl].inc(
                    e.dma_start(out=w2st[sl][:], in_=w2t_h[128 * hc:128 * (hc + 1), :]), 16)
            blk.sync(_l)

            aq = [None]

            def _q1(s, sl=sl, lt=lt, aq=aq):
                _w(s, lt[0], bv2_tag[0], hq_stage_free[sl])
                s.activation(out=ln2f[sl][:], in_=w2st[sl][:],
                             func=AF.Identity, bias=0.0, scale=bvec[:, 7:8])
                aq[0] = sfence(s)
            blk.scalar(_q1)
            w2_free[sl] = se.tag()

            def _q2(v, sl=sl, hc=hc, aq=aq):
                _w(v, aq[0])
                v.tensor_scalar(out=w2qT[:, 1024 * hc:1024 * (hc + 1)],
                                in0=ln2f[sl][:], scalar1=MAGIC, scalar2=-MAGIC,
                                op0=OP.add, op1=OP.add)
                vcar(v) if hc < 31 else vfence(v)
            blk.vector(_q2)
            hq_stage_free[sl] = ve.tag()
        w2q_done = ve.tag()

        if STOP == "w2q":
            barrier()
            finish(dumps=(w2qT[:, 0:16384],))
            return nc

        groups = [(0, 3), (3, 3), (6, 3), (9, 3), (12, 3), (15, 3), (18, 3),
                  (21, 3), (24, 3), (27, 3), (30, 2)]
        hr_free = [None] * 4
        hqf_free = [None] * 4
        hq_free = [None] * 4
        x1b_free = [None, None]
        out_free = [None, None, None, None]
        gg_free = [None, None, None]
        for gi, (t0, ntl) in enumerate(groups):
            mm_done = [None]
            for hc in range(32):
                sl = hc % 4
                lt = [None]

                def _l(e, sl=sl, hc=hc, t0=t0, ntl=ntl, lt=lt):
                    _w(e, hr_free[sl])
                    _w(e, s_hr[sl].tag())
                    lt[0] = s_hr[sl].inc(
                        e.dma_start(out=hrel[sl][:, 0:128 * ntl],
                                    in_=hscr[128 * hc:128 * (hc + 1),
                                             128 * t0:128 * (t0 + ntl)]), 16)
                blk.sync(_l)

                aq = [None]

                def _hq1(s, sl=sl, ntl=ntl, lt=lt, aq=aq):
                    _w(s, lt[0], hqf_free[sl])
                    s.activation(out=hqf[sl][:, 0:128 * ntl],
                                 in_=hrel[sl][:, 0:128 * ntl],
                                 func=AF.Identity, bias=0.0,
                                 scale=bvec[:, 4:5])
                    aq[0] = sfence(s)
                blk.scalar(_hq1)
                hr_free[sl] = se.tag()

                rq = [None]

                def _hq2(v, sl=sl, ntl=ntl, aq=aq, rq=rq):
                    _w(v, aq[0], hq_free[sl])
                    v.tensor_scalar(out=hqtl[sl][:, 0:128 * ntl],
                                    in0=hqf[sl][:, 0:128 * ntl],
                                    scalar1=MAGIC, scalar2=-MAGIC,
                                    op0=OP.add, op1=OP.add)
                    rq[0] = vfence(v)
                blk.vector(_hq2)
                hqf_free[sl] = ve.tag()

                def _mm2(tn, sl=sl, hc=hc, ntl=ntl, rq=rq, gi=gi):
                    _w(tn, rq[0], w2q_done if gi == 0 and hc == 0 else None)
                    if hc == 0:
                        for j in range(ntl):
                            _w(tn, gg_free[j])
                    for j in range(ntl):
                        for hf in range(2):
                            ii = tn.matmul(
                                out=GG[j][:, 512 * hf:512 * hf + 512],
                                lhsT=hqtl[sl][:, 128 * j:128 * (j + 1)],
                                rhs=w2qT[:, 1024 * hc + 512 * hf:1024 * hc + 512 * hf + 512],
                                start=(hc == 0), stop=False)
                    pe.inc(ii)
                blk.tensor(_mm2)
                hq_free[sl] = pe.tag()
            mmb = [None]

            def _b2m(tn, ntl=ntl, mmb=mmb):
                for j in range(ntl):
                    for hf in range(2):
                        ii = tn.matmul(out=GG[j][:, 512 * hf:512 * hf + 512],
                                       lhsT=onesf[0:1, :],
                                       rhs=b2r[0:1, 512 * hf:512 * hf + 512],
                                       start=False, stop=True)
                mmb[0] = pe.inc(ii)
            blk.tensor(_b2m)

            for j in range(ntl):
                t_ = t0 + j
                sl = j % 2
                lt = [None]

                def _lx(e, sl=sl, t_=t_, lt=lt):
                    _w(e, x1b_free[sl])
                    _w(e, s_x1r[sl].tag())
                    lt[0] = s_x1r[sl].inc(
                        e.dma_start(out=x1r2[sl][:], in_=x1scr[128 * t_:128 * (t_ + 1), :]), 16)
                blk.sync(_lx)

                et = [None]

                def _ev(v, j=j, sl=sl, lt=lt, mmb=mmb, et=et):
                    _w(v, mmb[0], lt[0])
                    for hf in range(2):
                        osl = (2 * j + hf) % 4
                        if out_free[osl] is not None:
                            _w(v, out_free[osl])
                        v.scalar_tensor_tensor(
                            out=outt[osl][:], in0=GG[j][:, 512 * hf:512 * hf + 512],
                            scalar=bvec[:, 5:6],
                            in1=x1r2[sl][:, 512 * hf:512 * hf + 512],
                            op0=OP.mult, op1=OP.add)
                    et[0] = vfence(v)
                blk.vector(_ev)

                def _so(s2, t_=t_, j=j, et=et):
                    _w(s2, et[0])
                    for hf in range(2):
                        osl = (2 * j + hf) % 4
                        _w(s2, s_out[osl].tag())
                        s_out[osl].inc(s2.dma_start(
                            out=out_h[128 * t_:128 * (t_ + 1), 512 * hf:512 * hf + 512],
                            in_=outt[osl][:]), 16)
                blk.scalar(_so)
                for hf in range(2):
                    osl = (2 * j + hf) % 4
                    out_free[osl] = s_out[osl].tag()
                x1b_free[sl] = ve.tag()
                gg_free[j] = et[0]

        def _fin(e):
            for q in s_out:
                _w(e, q.tag())
        blk.sync(_fin)
        f2ps.close()
        f2es.close()
        fesb.close()
    return nc


_CACHE = {}


def _get_nc():
    key = ("nc", STOP, DEBUG)
    if key not in _CACHE:
        _CACHE[key] = build()
    return _CACHE[key]


def _consts():
    if "consts" not in _CACHE:
        idb = np.eye(128, dtype=ml_dtypes.bfloat16)
        idf = np.eye(128, dtype=np.float32)
        kk = np.arange(128)[:, None]
        r = np.arange(128)[None, :]
        ut = (kk >= r).astype(np.float32)
        lt = (kk <= r).astype(np.float32)
        on = np.ones((128, 128), np.float32)
        m3 = np.concatenate([ut, on, lt], axis=1)
        maskt = np.concatenate([m3, m3], axis=1).astype(ml_dtypes.bfloat16)
        _CACHE["consts"] = (idb, idf, maskt)
    return _CACHE["consts"]


def kernel(L0_attn_x, L0_attn_Wq, L0_attn_bq, L0_attn_Wk, L0_attn_bk,
           L0_attn_Wv, L0_attn_bv, L0_attn_Wo, L0_attn_bo,
           L0_ffn_w1, L0_ffn_b1, L0_ffn_w2, L0_ffn_b2):
    nc = _get_nc()
    idb, idf, maskt = _consts()
    x = np.ascontiguousarray(np.asarray(L0_attn_x, dtype=np.float32))
    com = {
        "wq": np.ascontiguousarray(np.asarray(L0_attn_Wq, np.float32)),
        "wk": np.ascontiguousarray(np.asarray(L0_attn_Wk, np.float32)),
        "wv": np.ascontiguousarray(np.asarray(L0_attn_Wv, np.float32)),
        "wo": np.ascontiguousarray(np.asarray(L0_attn_Wo, np.float32)),
        "w1t": np.ascontiguousarray(np.asarray(L0_ffn_w1, np.float32).T),
        "w2t": np.ascontiguousarray(np.asarray(L0_ffn_w2, np.float32).T),
        "bq2": np.ascontiguousarray(np.asarray(L0_attn_bq, np.float32).reshape(8, 128).T),
        "bk2": np.ascontiguousarray(np.asarray(L0_attn_bk, np.float32).reshape(8, 128).T),
        "bv2": np.ascontiguousarray(np.asarray(L0_attn_bv, np.float32).reshape(8, 128).T),
        "bor": np.asarray(L0_attn_bo, np.float32).reshape(1, D),
        "b12": np.ascontiguousarray(np.asarray(L0_ffn_b1, np.float32).reshape(32, 128).T),
        "b2r": np.asarray(L0_ffn_b2, np.float32).reshape(1, D),
        "idb": idb, "idf": idf, "maskt": maskt,
    }
    in_maps = []
    for c in range(NCORES):
        m = dict(com)
        m["x"] = np.ascontiguousarray(x[2 * c:2 * c + 2].reshape(T, D))
        m["w1s"] = np.ascontiguousarray(com["w1t"][128 * c:128 * (c + 1), :])
        m["w2s"] = np.ascontiguousarray(com["w2t"][512 * c:512 * (c + 1), :])
        in_maps.append(m)
    res = run_bass_kernel_spmd(nc, in_maps, core_ids=list(range(NCORES)))
    out = np.empty((16, S, D), np.float32)
    for c in range(NCORES):
        out[2 * c:2 * c + 2] = np.asarray(res.results[c]["out"]).reshape(2, S, D)
    return out
